# version 08
# baseline (speedup 1.0000x reference)
import numpy as np
import ml_dtypes
from contextlib import ExitStack

import concourse.bass as bass
import concourse.mybir as mybir
from concourse.bass_utils import run_bass_kernel_spmd

F32 = mybir.dt.float32
BF16 = mybir.dt.bfloat16
AF = mybir.ActivationFunctionType
ALU = mybir.AluOpType
AX = mybir.AxisListType.X

D = 1024
TPC = 4096
NCH = 32
NT = 8
EPS_LN = 1e-5
EPS_AT = 1e-6
CLAMP_MAX = float(np.log(100.0))
WPD = 276

_CACHE = {}
_DBG = {}


def build_nc():
    nc = bass.Bass("TRN2", target_bir_lowering=False, detect_race_conditions=False)

    xh = nc.dram_tensor("x", [TPC, D], F32, kind="ExternalInput")
    wqh = nc.dram_tensor("wq", [D, D], F32, kind="ExternalInput")
    wkh = nc.dram_tensor("wk", [D, D], F32, kind="ExternalInput")
    wvh = nc.dram_tensor("wv", [D, D], F32, kind="ExternalInput")
    woh = nc.dram_tensor("wo", [D, D], F32, kind="ExternalInput")
    bqh = nc.dram_tensor("bq", [D], F32, kind="ExternalInput")
    bkh = nc.dram_tensor("bk", [D], F32, kind="ExternalInput")
    bvh = nc.dram_tensor("bv", [D], F32, kind="ExternalInput")
    boh = nc.dram_tensor("bo", [D], F32, kind="ExternalInput")
    rwh = nc.dram_tensor("rw", [16, 64, 64], F32, kind="ExternalInput")
    wh = nc.dram_tensor("w", [D, D], F32, kind="ExternalInput")
    wph = nc.dram_tensor("wp", [256, D], F32, kind="ExternalInput")
    bph = nc.dram_tensor("bp", [256], F32, kind="ExternalInput")
    simh = nc.dram_tensor("sim", [256, 4], F32, kind="ExternalInput")
    temph = nc.dram_tensor("temp", [1], F32, kind="ExternalInput")
    wdh = nc.dram_tensor("wd", [15, D], F32, kind="ExternalInput")
    wuh = nc.dram_tensor("wu", [D, 15], F32, kind="ExternalInput")
    identb_h = nc.dram_tensor("identb", [128, 128], BF16, kind="ExternalInput")
    identf_h = nc.dram_tensor("identf", [128, 128], F32, kind="ExternalInput")
    onesr_b_h = nc.dram_tensor("onesr_b", [1, 128], BF16, kind="ExternalInput")
    onesr_f_h = nc.dram_tensor("onesr_f", [1, 128], F32, kind="ExternalInput")
    onesc_b_h = nc.dram_tensor("onesc_b", [128, 1], BF16, kind="ExternalInput")
    onesc_f_h = nc.dram_tensor("onesc_f", [128, 1], F32, kind="ExternalInput")
    sel16_h = nc.dram_tensor("sel16", [16, 1024], BF16, kind="ExternalInput")
    yh = nc.dram_tensor("y", [TPC, D], F32, kind="ExternalOutput")

    ccin = nc.dram_tensor("ccin", [128, 1032], F32, kind="Internal")
    ccout = nc.dram_tensor("ccout", [128, 1032], F32, kind="Internal")

    with ExitStack() as ctx:
        e = ctx.enter_context

        xn = e(nc.sbuf_tensor("xn", [128, 8 * TPC], BF16))
        wq = e(nc.sbuf_tensor("wq_b", [128, 8 * 1024], BF16))
        wk = e(nc.sbuf_tensor("wk_b", [128, 8 * 1024], BF16))
        wv = e(nc.sbuf_tensor("wv_b", [128, 8 * 1024], BF16))
        wo = e(nc.sbuf_tensor("wo_b", [128, 8 * 1024], BF16))
        wpd = e(nc.sbuf_tensor("wpd", [128, 8 * WPD], BF16))
        wut = e(nc.sbuf_tensor("wut", [16, 1024], BF16))
        wu_st = e(nc.sbuf_tensor("wu_st", [128, 120], BF16))
        rw2 = e(nc.sbuf_tensor("rw2", [128, 1024], BF16))
        sel16 = e(nc.sbuf_tensor("sel16_s", [16, 1024], BF16))
        kv2 = e(nc.sbuf_tensor("kv2", [128, 1024], BF16))
        kbd = e(nc.sbuf_tensor("kbd", [128, 128], BF16))
        identb = e(nc.sbuf_tensor("identb_s", [128, 128], BF16))
        identf = e(nc.sbuf_tensor("identf_s", [128, 128], F32))
        onesr_b = e(nc.sbuf_tensor("onesr_b_s", [1, 128], BF16))
        onesr_f = e(nc.sbuf_tensor("onesr_f_s", [1, 128], F32))
        onesc_b = e(nc.sbuf_tensor("onesc_b_s", [128, 1], BF16))
        onesc_f = e(nc.sbuf_tensor("onesc_f_s", [128, 1], F32))
        bqs = e(nc.sbuf_tensor("bqs", [128, 8], F32))
        bks = e(nc.sbuf_tensor("bks", [128, 8], F32))
        bvbc = e(nc.sbuf_tensor("bvbc", [128, 1024], F32))
        bo_row = e(nc.sbuf_tensor("bo_row", [1, 1024], BF16))
        bp_row = e(nc.sbuf_tensor("bp_row", [1, WPD], BF16))
        rn4bc = e(nc.sbuf_tensor("rn4bc", [128, 4], F32))
        sclbc = e(nc.sbuf_tensor("sclbc", [128, 1], F32))
        cst = e(nc.sbuf_tensor("cst", [128, 2], F32))
        xtm = [e(nc.sbuf_tensor(f"xtm{i}", [128, 1024], F32)) for i in range(3)]
        sumx = e(nc.sbuf_tensor("sumx", [128, 2], F32))
        sumsq = e(nc.sbuf_tensor("sumsq", [128, 2], F32))
        meanv = e(nc.sbuf_tensor("meanv", [128, 2], F32))
        msq = e(nc.sbuf_tensor("msq", [128, 2], F32))
        varv = e(nc.sbuf_tensor("varv", [128, 2], F32))
        lnt = e(nc.sbuf_tensor("lnt", [128, 2], F32))
        rstd = e(nc.sbuf_tensor("rstd", [128, 2], F32))
        kq = e(nc.sbuf_tensor("kq", [128, 8 * 512], BF16))
        kpbuf = e(nc.sbuf_tensor("kpbuf", [128, 1024], BF16))
        kp_b = [kpbuf[:, i * 128:(i + 1) * 128] for i in range(8)]
        v_b = e(nc.sbuf_tensor("v_b", [128, 1024], BF16))
        qp_b = e(nc.sbuf_tensor("qp_b", [128, 8 * 512], BF16))
        o_b = e(nc.sbuf_tensor("o_b", [128, 8 * 512], BF16))
        norm_s = e(nc.sbuf_tensor("norm_s", [16, 512], F32))
        rn_b = e(nc.sbuf_tensor("rn_b", [16, 512], BF16))
        rnb_sb = [e(nc.sbuf_tensor(f"rnb{i}", [128, 512], F32)) for i in range(2)]
        out_tm = [e(nc.sbuf_tensor(f"outtm{i}", [128, 1040], F32)) for i in range(2)]
        xnt = [o_b[0:128, 2048:3072], o_b[0:128, 3072:4096]]
        sqsc = o_b[0:128, 0:1024]
        wd_bf = o_b[0:15, 1024:2048]
        kvstage = out_tm[0][0:128, 0:1032]
        kvpart = out_tm[1][0:128, 0:1032]
        bv_row = out_tm[0][0:1, 0:1024]
        m1 = e(nc.sbuf_tensor("m1", [128, 1], F32))
        lnp = e(nc.sbuf_tensor("lnp", [128, 1], F32))
        rnorm = e(nc.sbuf_tensor("rnorm", [128, 1], F32))
        rns = e(nc.sbuf_tensor("rns", [128, 1], F32))
        lg1 = e(nc.sbuf_tensor("lg1", [128, 4], F32))
        logits = e(nc.sbuf_tensor("logits", [128, 4], F32))
        nmx = e(nc.sbuf_tensor("nmx", [128, 1], F32))
        exps = e(nc.sbuf_tensor("exps", [128, 4], F32))
        sev = e(nc.sbuf_tensor("sev", [128, 1], F32))
        rse = e(nc.sbuf_tensor("rse", [128, 1], F32))
        lw = e(nc.sbuf_tensor("lw", [128, 4], F32))
        mxv = e(nc.sbuf_tensor("mxv", [128, 1], F32))
        maskv = e(nc.sbuf_tensor("maskv", [128, 4], F32))
        retv = e(nc.sbuf_tensor("retv", [128, 4], F32))
        sqg = e(nc.sbuf_tensor("sqg", [128, 256], BF16))
        ssq = e(nc.sbuf_tensor("ssq", [128, 1], F32))
        t1b = e(nc.sbuf_tensor("t1b", [128, 16], BF16))
        tpt = e(nc.sbuf_tensor("tpt", [16, 128], BF16))
        bq_st = e(nc.sbuf_tensor("bq_st", [8, 128], F32))
        bk_st = e(nc.sbuf_tensor("bk_st", [8, 128], F32))
        bp2 = e(nc.sbuf_tensor("bp2", [2, 128], F32))
        bp2t = e(nc.sbuf_tensor("bp2t", [128, 2], F32))
        sim_f = e(nc.sbuf_tensor("sim_f", [128, 8], F32))
        sim_b = e(nc.sbuf_tensor("sim_b", [128, 8], BF16))
        sim_sq = e(nc.sbuf_tensor("sim_sq", [128, 8], F32))
        s4 = e(nc.sbuf_tensor("s4", [1, 4], F32))
        s4m = e(nc.sbuf_tensor("s4m", [1, 4], F32))
        tmp4 = e(nc.sbuf_tensor("tmp4", [1, 4], F32))
        rn4 = e(nc.sbuf_tensor("rn4", [1, 4], F32))
        scl = e(nc.sbuf_tensor("scl", [1, 1], F32))
        sclm = e(nc.sbuf_tensor("sclm", [1, 1], F32))
        sclE = e(nc.sbuf_tensor("sclE", [1, 1], F32))

        SEMN = ["s_pl", "s_ax", "s_ag", "s_ad", "s_aa", "s_ap", "s_1p", "s_1d",
                "s_cst", "s_xd", "s_xg", "s_cc",
                "s_2p", "s_2d", "s_2a", "s_2x", "s_2s"]
        sem = {n: e(nc.semaphore(n)) for n in SEMN}

        xn_r = xn[:, :].rearrange("p (c t) -> p c t", c=8)

        with nc.psum_tensor("pa0", [128, 512], F32) as pa0, \
             nc.psum_tensor("p1k0", [128, 512], F32) as p1k0, \
             nc.psum_tensor("p1k1", [128, 512], F32) as p1k1, \
             nc.psum_tensor("p1v", [128, 1024], F32) as p1v, \
             nc.psum_tensor("kvA", [128, 512], F32) as kvA, \
             nc.psum_tensor("kvB", [128, 512], F32) as kvB, \
             nc.psum_tensor("kvS", [128, 512], F32) as kvS:
            B1, B2 = p1k0, p1k1
            p1k = [p1k0, p1k1]
            pab = pa0[:, :].bitcast(BF16)
            pab_r = pab.rearrange("p (c t) -> p c t", c=8)
            B1b = B1[:, :].bitcast(BF16)
            B2b = B2[:, :].bitcast(BF16)

            C = {n: {"n": 0, "m": {}} for n in SEMN}
            streams = {k: [] for k in ["sync", "gpsimd", "tensor", "vector", "scalar"]}

            def op(eng, fn, waits=(), inc=None, by=1):
                streams[eng].append((fn, list(waits), inc, by))

            def cdma(dst, src):
                return lambda g: g.dma_start(out=dst, in_=src)

            consts = [
                ("identb", identb[:, :], identb_h[:, :]),
                ("identf", identf[:, :], identf_h[:, :]),
                ("onesr_b", onesr_b[:, :], onesr_b_h[:, :]),
                ("onesr_f", onesr_f[:, :], onesr_f_h[:, :]),
                ("onesc_b", onesc_b[:, :], onesc_b_h[:, :]),
                ("onesc_f", onesc_f[:, :], onesc_f_h[:, :]),
                ("sel16", sel16[:, :], sel16_h[:, :]),
                ("bq_st", bq_st[:, :], bqh.ap().rearrange("(c j) -> c j", c=8)),
                ("bk_st", bk_st[:, :], bkh.ap().rearrange("(c j) -> c j", c=8)),
                ("bp2", bp2[:, :], bph.ap().rearrange("(c j) -> c j", c=2)),
                ("bv_row", bv_row[:, :], bvh.ap().rearrange("(a n) -> a n", a=1)),
                ("sim0", sim_f[:, 0:4], simh[0:128, :]),
                ("sim1", sim_f[:, 4:8], simh[128:256, :]),
                ("scl", scl[:, :], temph.ap().rearrange("(a n) -> a n", a=1)),
            ]
            for nm, d, s_ in consts:
                op("sync", cdma(d, s_), inc=("s_pl", ("c", nm)), by=16)
            for i in range(NCH):
                w = []
                if i >= 3:
                    w = [("s_ad", ("app", i - 3)), ("s_aa", ("sq", i - 3))]
                op("sync", cdma(xtm[i % 3][:, :], xh[i * 128:(i + 1) * 128, :]),
                   waits=w, inc=("s_ax", ("x", i)), by=16)

            def ms(dst, val):
                return lambda g: g.memset(dst, val)

            op("gpsimd", ms(cst[:, 0:1], EPS_LN))
            op("gpsimd", ms(cst[:, 1:2], 1e-24), inc=("s_cst", ("cst1",)))
            for d, v in [(rw2[:, :], 0.0), (kv2[:, :], 0.0), (kbd[:, :], 0.0),
                         (t1b[:, 15:16], 0.0), (wpd[:, :], 0.0),
                         (bp_row[:, :], 0.0)]:
                op("gpsimd", ms(d, v))
            op("gpsimd", ms(wut[:, :], 0.0), inc=("s_cst", ("msets",)))
            op("gpsimd", cdma(bp_row[0:1, 0:256], bph.ap().rearrange("(a n) -> a n", a=1)),
               inc=("s_ag", ("bpr",)), by=16)
            for oc in range(8):
                op("gpsimd", cdma(wu_st[:, oc * 15:(oc + 1) * 15],
                                  wuh[oc * 128:(oc + 1) * 128, :]),
                   inc=("s_ag", ("wu", oc)), by=16)
            for pc in range(2):
                op("gpsimd", cdma(qp_b[:, pc * 1024:(pc + 1) * 1024],
                                  wph[pc * 128:(pc + 1) * 128, :]),
                   inc=("s_ag", ("wp", pc)), by=16)
            op("gpsimd", cdma(bo_row[:, :], boh.ap().rearrange("(a n) -> a n", a=1)),
               inc=("s_ag", ("bo",)), by=16)
            op("gpsimd", cdma(wd_bf[:, :], wdh[:, :]), inc=("s_ag", ("wd",)), by=16)
            for nm, hh in [("wk", wkh), ("wv", wvh)]:
                dst = {"wk": wk, "wv": wv}[nm]
                for c in range(8):
                    op("gpsimd", cdma(dst[:, c * 1024:(c + 1) * 1024],
                                      hh[c * 128:(c + 1) * 128, :]),
                       inc=("s_ag", (nm, c)), by=16)
            for p in range(8):
                op("gpsimd", cdma(rw2[0:64, p * 128:p * 128 + 64], rwh[2 * p]),
                   inc=("s_ag", ("rw", 2 * p)), by=16)
                op("gpsimd", cdma(rw2[64:128, p * 128 + 64:p * 128 + 128], rwh[2 * p + 1]),
                   inc=("s_ag", ("rw", 2 * p + 1)), by=16)
            for nm, hh in [("wq", wqh), ("wo", woh)]:
                dst = {"wq": wq, "wo": wo}[nm]
                for c in range(8):
                    op("gpsimd", cdma(dst[:, c * 1024:(c + 1) * 1024],
                                      hh[c * 128:(c + 1) * 128, :]),
                       inc=("s_ag", (nm, c)), by=16)
            for c in range(8):
                w = [("s_ap", ("wps",))] if c < 2 else [("s_1p", ("wt", c - 2))]
                op("gpsimd", cdma(qp_b[:, (c % 2) * 1024:(c % 2 + 1) * 1024],
                                  wh[c * 128:(c + 1) * 128, :]),
                   waits=w, inc=("s_ag", ("W", c)), by=16)
            op("gpsimd", cdma(ccin[:, :], kvstage[:, :]),
               waits=[("s_xd", ("kvst",))], inc=("s_xg", ("ccin",)), by=16)

            def coll(g):
                g.collective_compute(
                    "AllReduce", ALU.add,
                    replica_groups=[[0, 1], [2, 3], [4, 5], [6, 7]],
                    ins=[ccin.ap().opt()], outs=[ccout.ap().opt()],
                ).then_inc(sem["s_cc"])
                return None
            op("gpsimd", coll, waits=[("s_xg", ("ccin",))])

            def ln_grp(i):
                par = slice(i % 2, i % 2 + 1)

                def red(v, i=i, par=par):
                    return v.reduce_sum(sumx[:, par], xtm[i % 3][:, :], axis=AX)
                op("vector", red, waits=[("s_ax", ("x", i))], inc=("s_ad", ("red", i)))

                def mean(v, i=i, par=par):
                    return v.tensor_scalar(meanv[:, par], sumx[:, par], 1.0 / D, None, ALU.mult)
                op("vector", mean, waits=[("s_ad", ("red", i))], inc=("s_ad", ("mean", i)))

                def msqf(v, i=i, par=par):
                    return v.tensor_mul(msq[:, par], meanv[:, par], meanv[:, par])
                op("vector", msqf, waits=[("s_ad", ("mean", i))], inc=("s_ad", ("msq", i)))

                def varf(v, i=i, par=par):
                    return v.scalar_tensor_tensor(varv[:, par], sumsq[:, par], 1.0 / D,
                                                  msq[:, par], ALU.mult, ALU.subtract)
                op("vector", varf, waits=[("s_ad", ("msq", i)), ("s_aa", ("sq", i))],
                   inc=("s_ad", ("var", i)))
                if i >= 1:
                    j = i - 1
                    parj = slice(j % 2, j % 2 + 1)

                    def app(v, j=j, parj=parj):
                        return v.tensor_scalar(xnt[j % 2][:, :], xtm[j % 3][:, :],
                                               meanv[:, parj], rstd[:, parj],
                                               ALU.subtract, ALU.mult)
                    w = [("s_aa", ("rstd", j))]
                    if j >= 2:
                        w.append(("s_ap", ("T", j - 2)))
                    op("vector", app, waits=w, inc=("s_ad", ("app", j)))

            def vcopy(dst, src, waits, inc):
                op("vector", (lambda v, d=dst, s_=src: v.tensor_copy(d, s_)),
                   waits=waits, inc=inc)

            def prep_d(stage):
                if stage == 1:
                    vcopy(bvbc[:, 0:512], B1[:, :], [("s_ap", ("bvmm",))], ("s_ad", ("bvbc0",)))
                    vcopy(bvbc[:, 512:1024], B2[:, :], [], ("s_ad", ("bvbc",)))
                elif stage == 2:
                    for oc in range(4):
                        vcopy(wut[0:15, oc * 128:(oc + 1) * 128],
                              B2b[0:15, oc * 128:(oc + 1) * 128],
                              [("s_ap", ("wutT", oc)), ("s_cst", ("msets",))],
                              ("s_ad", ("wutev", oc)))
                elif stage == 22:
                    for oc in range(4, 8):
                        vcopy(wut[0:15, oc * 128:(oc + 1) * 128],
                              B2b[0:15, oc * 128:(oc + 1) * 128],
                              [("s_ap", ("wutT", oc))], ("s_ad", ("wutev", oc)))
                elif stage == 3:
                    for pc in range(2):
                        src = (B1b if pc == 0 else B2b)
                        wt = [("s_ap", ("wpt0",)), ("s_cst", ("msets",))] if pc == 0 \
                            else [("s_ap", ("wpt1",))]
                        for dc in range(8):
                            vcopy(wpd[:, dc * WPD + pc * 128: dc * WPD + (pc + 1) * 128],
                                  src[:, dc * 128:(dc + 1) * 128],
                                  wt if dc == 0 else [],
                                  ("s_ad", (f"wpt{pc}ev", dc)))
                    vcopy(sim_b[:, :], sim_f[:, :], [("s_pl", ("c", "sim1"))], ("s_ad", ("simc",)))

                    def simsq(v):
                        return v.tensor_mul(sim_sq[:, :], sim_f[:, :], sim_f[:, :])
                    op("vector", simsq, waits=[], inc=("s_ad", ("simsq",)))
                elif stage == 4:
                    for dc in range(8):
                        vcopy(wpd[:, dc * WPD + 256: dc * WPD + 271],
                              B1b[:, dc * 16: dc * 16 + 15],
                              [("s_ap", ("wdt",))] if dc == 0 else [],
                              ("s_ad", ("wdtev", dc)))
                    vcopy(bqs[:, :], B2[:, 0:8], [], ("s_ad", ("bqsev",)))
                    vcopy(bks[:, :], B2[:, 8:16], [], ("s_ad", ("bksev",)))
                    vcopy(bp2t[:, :], B2[:, 16:18], [], ("s_ad", ("bp2tev",)))
                elif stage == 5:
                    for dc in range(8):
                        vcopy(wpd[:, dc * WPD + 271: dc * WPD + 275],
                              B1[:, dc * 4:(dc + 1) * 4],
                              [("s_ap", ("wps",))] if dc == 0 else [],
                              ("s_ad", ("wpsev", dc)))
                    vcopy(bp_row[:, 271:275], B2[0:1, 0:4], [], ("s_ad", ("bprow1",)))
                    vcopy(s4[:, :], B2[0:1, 8:12], [], ("s_ad", ("s4copy",)))

                    def sclmf(v):
                        return v.tensor_scalar(sclm[:, :], scl[:, :], CLAMP_MAX, None, ALU.min)
                    op("vector", sclmf, waits=[("s_pl", ("c", "scl"))], inc=("s_ad", ("sclm",)))

                    def s4mf(v):
                        return v.tensor_scalar(s4m[:, :], s4[:, :], 1e-24, None, ALU.max)
                    op("vector", s4mf, waits=[("s_ad", ("s4copy",))], inc=("s_ad", ("s4m",)))
                elif stage == 6:
                    vcopy(rn4bc[:, :], B1[:, 0:4], [("s_ap", ("rnscl",))], ("s_ad", ("rn4ev",)))
                    vcopy(sclbc[:, :], B1[:, 8:9], [], ("s_ad", ("prepdone",)))

            def p3_s(t, s):
                w = [("s_1p", ("V", t, s))]
                if s > 0:
                    w.append(("s_1p", ("kv", t, s - 1, 7)))
                elif t > 0:
                    w.append(("s_1p", ("kv", t - 1, 3, 7)))

                def vb(v, t=t, s=s):
                    return v.tensor_tensor(v_b[:, :], p1v[:, :], bvbc[:, :], ALU.add)
                op("vector", vb, waits=w, inc=("s_1d", ("vb", t, s)))
                for half in range(2):
                    w = [("s_1p", ("kpA" if half == 0 else "kpB", t, s))]
                    if s > 0:
                        w.append(("s_1p", ("kv", t, s - 1, 3 + 4 * half)))
                    elif t > 0:
                        w.append(("s_1p", ("kv", t - 1, 3, 3 + 4 * half)))

                    def relu(v, t=t, s=s, half=half):
                        return v.tensor_scalar(kpbuf[:, half * 512:(half + 1) * 512],
                                               p1k[half][:, :],
                                               0.0, EPS_AT, ALU.max, ALU.add)
                    op("vector", relu, waits=w,
                       inc=("s_1d", ("reluA" if half == 0 else "reluB", t, s)))

            def dve_kq(t):
                for c in range(8):
                    def kqf(v, t=t, c=c):
                        return v.tensor_scalar(kq[:, c * 512:(c + 1) * 512], p1k[c % 2][:, :],
                                               bks[:, c:c + 1], None, ALU.add)
                    w = [("s_1p", ("K", t, c))]
                    if t > 0:
                        w.append(("s_1p", ("kpA" if c < 4 else "kpB", t - 1, 3)))
                    if t == 0 and c == 0:
                        w.append(("s_ad", ("bksev",)))
                    op("vector", kqf, waits=w, inc=("s_ad", ("kq", t, c)))

            hooks = {1: [1], 3: [2], 4: [22], 5: [3], 6: [4], 7: [5, 6]}
            for i in range(9):
                ln_grp(i)
                for st in hooks.get(i, []):
                    prep_d(st)
            dve_kq(0)

            def app31():
                j = NCH - 1
                parj = slice(j % 2, j % 2 + 1)

                def f(v, j=j, parj=parj):
                    return v.tensor_scalar(xnt[j % 2][:, :], xtm[j % 3][:, :],
                                           meanv[:, parj], rstd[:, parj],
                                           ALU.subtract, ALU.mult)
                op("vector", f, waits=[("s_aa", ("rstd", j)), ("s_ap", ("T", j - 2))],
                   inc=("s_ad", ("app", j)))

            for t in range(NT):
                p3_s(t, 0)
                for i in (4 * t + 9, 4 * t + 10):
                    if i < NCH:
                        ln_grp(i)
                p3_s(t, 1)
                for i in (4 * t + 11, 4 * t + 12):
                    if i < NCH:
                        ln_grp(i)
                        if i == NCH - 1:
                            app31()
                p3_s(t, 2)
                p3_s(t, 3)
                if t < NT - 1:
                    dve_kq(t + 1)
            vcopy(kvstage[:, 0:512], kvA[:, :], [("s_1p", ("kv", 7, 3, 7))], ("s_xd", ("kvA",)))
            vcopy(kvstage[:, 512:1024], kvB[:, :], [], ("s_xd", ("kvB",)))
            vcopy(kvstage[:, 1024:1032], kvS[:, 0:8], [], ("s_xd", ("kvst",)))
            for c in range(8):
                src = p1k[c % 2][:, :].bitcast(BF16).rearrange("p (cc t) -> p cc t", cc=8)
                wtr = wk[:, :].rearrange("p (cc o) -> p cc o", cc=8)

                def wkev0(v, c=c, src=src, wtr=wtr):
                    return v.tensor_copy(wtr[:, 0:4, c * 128:(c + 1) * 128], src[:, 0:4, :])

                def wkev1(v, c=c, src=src, wtr=wtr):
                    return v.tensor_copy(wtr[:, 4:8, c * 128:(c + 1) * 128], src[:, 4:8, :])
                op("vector", wkev0, waits=[("s_1p", ("wt", c))], inc=("s_1d", ("wkev0", c)))
                op("vector", wkev1, waits=[], inc=("s_1d", ("wkev", c)))

            def act_grp(i):
                par = slice(i % 2, i % 2 + 1)

                def sq(s_, i=i, par=par):
                    return s_.activation(sqsc[:, :], xtm[i % 3][:, :], AF.Square,
                                         accum_out=sumsq[:, par])
                w = [("s_ax", ("x", i))]
                if i >= 2:
                    w.append(("s_ad", ("var", i - 2)))
                op("scalar", sq, waits=w, inc=("s_aa", ("sq", i)))

                def lnf(s_, i=i, par=par):
                    return s_.activation(lnt[:, par], varv[:, par], AF.Ln, bias=cst[:, 0:1])
                w = [("s_ad", ("var", i))]
                if i == 0:
                    w.append(("s_cst", ("cst1",)))
                op("scalar", lnf, waits=w, inc=("s_aa", ("ln", i)))

                def rstdf(s_, i=i, par=par):
                    return s_.activation(rstd[:, par], lnt[:, par], AF.Exp, scale=-0.5)
                op("scalar", rstdf, waits=[("s_aa", ("ln", i))], inc=("s_aa", ("rstd", i)))
                if i >= 2:
                    act_ev(i - 2)

            def act_ev(i):
                def ev1(s_, i=i):
                    return s_.activation(xn_r[:, 0:4, i * 128:(i + 1) * 128],
                                         pab_r[:, 0:4, :], AF.Copy)

                def ev2(s_, i=i):
                    return s_.activation(xn_r[:, 4:8, i * 128:(i + 1) * 128],
                                         pab_r[:, 4:8, :], AF.Copy)
                op("scalar", ev1, waits=[("s_ap", ("T", i))], inc=("s_aa", ("ev1", i)))
                op("scalar", ev2, waits=[], inc=("s_aa", ("ev2", i)))

            for i in range(NCH):
                act_grp(i)
                if i == 7:
                    def tmp4f(s_):
                        return s_.activation(tmp4[:, :], s4m[:, :], AF.Ln)
                    op("scalar", tmp4f, waits=[("s_ad", ("s4m",))], inc=("s_aa", ("pln",)))

                    def rn4f(s_):
                        return s_.activation(rn4[:, :], tmp4[:, :], AF.Exp, scale=-0.5)
                    op("scalar", rn4f, waits=[("s_aa", ("pln",))], inc=("s_aa", ("prn4",)))

                    def sclEf(s_):
                        return s_.activation(sclE[:, :], sclm[:, :], AF.Exp)
                    op("scalar", sclEf, waits=[("s_ad", ("sclm",))], inc=("s_aa", ("pexp",)))
            act_ev(NCH - 2)
            act_ev(NCH - 1)

            def t_grp(i):
                def tr(t_, i=i):
                    for jj in range(8):
                        li = t_.transpose(pab[:, jj * 128:(jj + 1) * 128],
                                          xnt[i % 2][:, jj * 128:(jj + 1) * 128],
                                          identb[:, :])
                    return li
                w = [("s_ad", ("app", i))]
                if i >= 1:
                    w.append(("s_aa", ("ev2", i - 1)))
                if i == 0:
                    w.append(("s_pl", ("c", "identb")))
                op("tensor", tr, waits=w, inc=("s_ap", ("T", i)))

            def pe_prep_bvmm():
                def f(t_):
                    t_.matmul(B1[:, :], onesr_f[:, :], bv_row[0:1, 0:512], start=True, stop=True)
                    return t_.matmul(B2[:, :], onesr_f[:, :], bv_row[0:1, 512:1024],
                                     start=True, stop=True)
                op("tensor", f, waits=[("s_pl", ("c", "scl"))], inc=("s_ap", ("bvmm",)))

            def pe_wut(ocs):
                for oc in ocs:
                    def f(t_, oc=oc):
                        return t_.transpose(B2b[0:15, oc * 128:(oc + 1) * 128],
                                            wu_st[:, oc * 15:oc * 15 + 15],
                                            identb[:, :])
                    w = []
                    if oc == 0:
                        w = [("s_ag", ("wu", 7)), ("s_ad", ("bvbc",))]
                    op("tensor", f, waits=w, inc=("s_ap", ("wutT", oc)))

            def pe_wpt(pc):
                def f(t_, pc=pc):
                    bank = B1b if pc == 0 else B2b
                    for dc in range(8):
                        li = t_.transpose(bank[:, dc * 128:(dc + 1) * 128],
                                          qp_b[:, pc * 1024 + dc * 128: pc * 1024 + (dc + 1) * 128],
                                          identb[:, :])
                    return li
                if pc == 0:
                    w = [("s_ag", ("wp", 1)), ("s_ad", ("bvbc",))]
                else:
                    w = [("s_ad", ("wutev", 7))]
                op("tensor", f, waits=w, inc=("s_ap", (f"wpt{pc}",)))

            def pe_wdt():
                def f(t_):
                    for dc in range(8):
                        t_.transpose(B1b[:, dc * 16: dc * 16 + 15],
                                     wd_bf[0:15, dc * 128:(dc + 1) * 128], identb[0:15, 0:15])
                    t_.transpose(B2[:, 0:8], bq_st[:, :], identf[0:8, 0:8])
                    t_.transpose(B2[:, 8:16], bk_st[:, :], identf[0:8, 0:8])
                    return t_.transpose(B2[:, 16:18], bp2[:, :], identf[0:2, 0:2])
                op("tensor", f,
                   waits=[("s_ad", ("wpt0ev", 7)), ("s_ad", ("wpt1ev", 7)),
                          ("s_ag", ("wd",)), ("s_pl", ("c", "bp2"))],
                   inc=("s_ap", ("wdt",)))

            def pe_wps():
                def f(t_):
                    first = True
                    for dc in range(8):
                        for pc in range(2):
                            t_.matmul(B1[:, dc * 4:(dc + 1) * 4],
                                      qp_b[:, pc * 1024 + dc * 128: pc * 1024 + (dc + 1) * 128],
                                      sim_b[:, pc * 4:(pc + 1) * 4],
                                      start=first, stop=(dc == 7 and pc == 1))
                            first = False
                    t_.matmul(B2[0:1, 0:4], bp2t[:, 0:1], sim_f[:, 0:4], start=True, stop=False)
                    t_.matmul(B2[0:1, 0:4], bp2t[:, 1:2], sim_f[:, 4:8], start=False, stop=False)
                    t_.matmul(B2[0:1, 8:12], onesc_f[:, :], sim_sq[:, 0:4], start=False, stop=False)
                    return t_.matmul(B2[0:1, 8:12], onesc_f[:, :], sim_sq[:, 4:8],
                                     start=False, stop=True)
                op("tensor", f,
                   waits=[("s_ad", ("wdtev", 7)), ("s_ad", ("bp2tev",)),
                          ("s_ad", ("simc",)), ("s_ad", ("simsq",))],
                   inc=("s_ap", ("wps",)))

            def pe_rnscl():
                def f(t_):
                    t_.matmul(B1[:, 0:4], onesr_f[:, :], rn4[:, :], start=True, stop=False)
                    return t_.matmul(B1[:, 8:9], onesr_f[:, :], sclE[:, :], start=False, stop=True)
                op("tensor", f,
                   waits=[("s_aa", ("prn4",)), ("s_aa", ("pexp",)), ("s_ad", ("wpsev", 7))],
                   inc=("s_ap", ("rnscl",)))

            def pe_K(t, c):
                def f(t_, t=t, c=c):
                    for kc in range(8):
                        li = t_.matmul(p1k[c % 2][:, :],
                                       wk[:, kc * 1024 + c * 128: kc * 1024 + (c + 1) * 128],
                                       xn[:, kc * TPC + t * 512: kc * TPC + (t + 1) * 512],
                                       start=(kc == 0), stop=(kc == 7))
                    return li
                w = []
                if t == 0 and c == 0:
                    w += [("s_ad", ("prepdone",)), ("s_ag", ("wk", 7))]
                if t == 0 and c == 1:
                    w += [("s_ad", ("prepdone",))]
                if c == 0:
                    w.append(("s_aa", ("ev2", 4 * t + 3)))
                if c >= 2:
                    w.append(("s_ad", ("kq", t, c - 2)))
                elif t > 0:
                    w.append(("s_1d", ("reluA" if c == 0 else "reluB", t - 1, 3)))
                op("tensor", f, waits=w, inc=("s_1p", ("K", t, c)))

            def pe_V(t, s):
                def f(t_, t=t, s=s):
                    tok = t * 512 + s * 128
                    for kc in range(8):
                        for h in range(2):
                            li = t_.matmul(p1v[:, h * 512:(h + 1) * 512],
                                           xn[:, kc * TPC + tok: kc * TPC + tok + 128],
                                           wv[:, kc * 1024 + h * 512: kc * 1024 + (h + 1) * 512],
                                           start=(kc == 0), stop=(kc == 7))
                    return li
                w = []
                if t == 0 and s == 0:
                    w.append(("s_ag", ("wv", 7)))
                if s > 0:
                    w.append(("s_1d", ("vb", t, s - 1)))
                elif t > 0:
                    w.append(("s_1d", ("vb", t - 1, 3)))
                op("tensor", f, waits=w, inc=("s_1p", ("V", t, s)))

            def pe_kp_half(t, s, half):
                def f(t_, t=t, s=s, half=half):
                    for pp in range(4):
                        p = 4 * half + pp
                        li = t_.matmul(p1k[half][:, pp * 128:(pp + 1) * 128],
                                       kq[:, p * 512 + s * 128: p * 512 + (s + 1) * 128],
                                       rw2[:, p * 128:(p + 1) * 128],
                                       start=True, stop=True)
                    return li
                w = [("s_ad", ("kq", t, 4 * half + 3))]
                if s > 0:
                    w.append(("s_1d", ("reluA" if half == 0 else "reluB", t, s - 1)))
                else:
                    w.append(("s_ad", ("kq", t, 6 + half)))
                if t == 0 and s == 0 and half == 0:
                    w += [("s_ag", ("rw", 15)), ("s_cst", ("msets",))]
                op("tensor", f, waits=w, inc=("s_1p", ("kpA" if half == 0 else "kpB", t, s)))

            def pe_kv(t, s, p):
                def f(t_, t=t, s=s, p=p):
                    kvb = kvA if p < 4 else kvB
                    col = (p % 4) * 128
                    fr = (t == 0 and s == 0)
                    last = (t == NT - 1 and s == 3)
                    t_.matmul(kvb[:, col:col + 128], kp_b[p][:, :],
                              v_b[:, p * 128:(p + 1) * 128],
                              start=(fr and p % 4 == 0), stop=(last and p % 4 == 3))
                    return t_.matmul(kvS[:, p:p + 1], kp_b[p][:, :], onesc_b[:, :],
                                     start=(fr and p == 0), stop=(last and p == 7))
                w = []
                if p % 4 == 0:
                    w = [("s_1d", ("reluA" if p < 4 else "reluB", t, s)),
                         ("s_1d", ("vb", t, s))]
                op("tensor", f, waits=w, inc=("s_1p", ("kv", t, s, p)))

            def pe_wt(c):
                def f(t_, c=c):
                    dst = p1k[c % 2][:, :].bitcast(BF16)
                    for dd in range(8):
                        li = t_.transpose(dst[:, dd * 128:(dd + 1) * 128],
                                          qp_b[:, (c % 2) * 1024 + dd * 128:
                                               (c % 2) * 1024 + (dd + 1) * 128],
                                          identb[:, :])
                    return li
                w = [("s_ag", ("W", c))]
                if c < 2:
                    w.append(("s_1d", ("reluA" if c == 0 else "reluB", 7, 3)))
                else:
                    w.append(("s_1d", ("wkev", c - 2)))
                op("tensor", f, waits=w, inc=("s_1p", ("wt", c)))

            pe_prep_bvmm()
            pe_wut(range(4))
            t_grp(0)
            pe_wpt(0)
            t_grp(1)
            pe_wut(range(4, 8))
            t_grp(2)
            pe_wpt(1)
            t_grp(3)
            pe_wdt()
            t_grp(4)
            pe_wps()
            t_grp(5)
            pe_rnscl()
            t_grp(6)
            t_grp(7)
            def maybe_t(i):
                if i < NCH:
                    t_grp(i)

            for t in range(NT):
                for c in range(8):
                    pe_K(t, c)
                for s in range(4):
                    pe_V(t, s)
                    pe_kp_half(t, s, 0)
                    pe_kp_half(t, s, 1)
                    if s == 1:
                        maybe_t(4 * t + 8)
                    elif s == 2:
                        maybe_t(4 * t + 10)
                    for p in range(8):
                        pe_kv(t, s, p)
                    if s == 1:
                        maybe_t(4 * t + 9)
                    elif s == 2:
                        maybe_t(4 * t + 11)
            for c in range(8):
                pe_wt(c)

            _DBG["streams"] = streams
            for en in streams:
                for (fn, waits, inc, by) in streams[en]:
                    if inc is not None:
                        cc = C[inc[0]]
                        cc["n"] += by
                        cc["m"][inc[1]] = cc["n"]

            _DBG["C"] = C

            def emit(en, engine):
                for (fn, waits, inc, by) in streams[en]:
                    for (sn, key) in waits:
                        engine.wait_ge(sem[sn], C[sn]["m"][key])
                    ins = fn(engine)
                    if inc is not None:
                        ins.then_inc(sem[inc[0]], by)

            with nc.Block() as blk:
                @blk.sync
                def _(sy):
                    emit("sync", sy)

                @blk.gpsimd
                def _(g):
                    emit("gpsimd", g)

                @blk.vector
                def _(v):
                    emit("vector", v)

                @blk.scalar
                def _(s_):
                    emit("scalar", s_)

                @blk.tensor
                def _(t_):
                    emit("tensor", t_)

        s_2p, s_2d, s_2a = sem["s_2p"], sem["s_2d"], sem["s_2a"]
        s_2x, s_2s = sem["s_2x"], sem["s_2s"]
        s_xd, s_xg, s_cc = sem["s_xd"], sem["s_xg"], sem["s_cc"]
        XD_MEGA = C["s_xd"]["n"]
        XG_MEGA = C["s_xg"]["n"]

        nt4 = NT
        with nc.psum_tensor("P0", [128, 512], F32) as P0, \
             nc.psum_tensor("P1", [128, 512], F32) as P1, \
             nc.psum_tensor("P2", [128, 512], F32) as P2, \
             nc.psum_tensor("P3", [128, 512], F32) as P3, \
             nc.psum_tensor("P4", [128, 512], F32) as P4, \
             nc.psum_tensor("P5", [128, 512], F32) as P5, \
             nc.psum_tensor("PASM", [128, 1024], F32) as PASM:
            PQ = [P0, P1]
            PR = [P3, P4]
            PPRJ = [P2, P5]

            D2T, A2T, P2T = 98, 24, 45

            def d2(t, k):
                return D2T * t + k

            def a2(t, k):
                return A2T * t + k

            def p2(t, k):
                return P2T * t + k

            def dQ(c):
                return c + 1

            def dqp(p):
                return 8 + p + 1

            D_RN1, D_RN2 = 17, 18

            def ddiv(c):
                return 18 + c + 1

            def dg(tt, k):
                return 26 + 18 * tt + k

            def pQ(c):
                return c + 1

            def pqp(p):
                return 8 + p + 1

            P_NORM = 17

            def pqkv(c):
                return 17 + 2 * c + 1

            def prnb(c):
                return 17 + 2 * c + 2

            _PPRJ = {0: 34, 1: 35, 2: 38, 3: 41}
            _PTPT = {0: 36, 1: 39, 2: 42, 3: 44}
            _PASM = {0: 37, 1: 40, 2: 43, 3: 45}

            def pprj(tt):
                return _PPRJ[tt]

            def ptpt(tt):
                return _PTPT[tt]

            def pasm(tt):
                return _PASM[tt]

            def aRnb(c):
                return c + 1

            def ag(tt, k):
                return 8 + 4 * tt + k

            def asm_banks(tt):
                if tt % 2 == 1:
                    return PASM[:, 0:512], PASM[:, 512:1024]
                return P0[:, :], P1[:, :]

            with nc.Block() as blk:
                @blk.tensor
                def _(t_):
                    for t in range(nt4):
                        for c in range(8):
                            if c >= 2:
                                t_.wait_ge(s_2d, d2(t, dQ(c - 2)))
                            elif t > 0:
                                t_.wait_ge(s_2d, d2(t - 1, dg(3, 18)))
                            for kc in range(8):
                                li = t_.matmul(PQ[c % 2][:, :],
                                          wq[:, kc * 1024 + c * 128: kc * 1024 + (c + 1) * 128],
                                          xn[:, kc * TPC + t * 512: kc * TPC + (t + 1) * 512],
                                          start=(kc == 0), stop=(kc == 7))
                            li.then_inc(s_2p, 1)
                        for p in range(8):
                            t_.wait_ge(s_2d, d2(t, 8))
                            if p >= 2:
                                t_.wait_ge(s_2d, d2(t, dqp(p - 2)))
                            t_.matmul(PQ[p % 2][:, :], rw2[:, p * 128:(p + 1) * 128],
                                      kq[:, p * 512:(p + 1) * 512], start=True, stop=True).then_inc(s_2p, 1)
                        if t == 0:
                            t_.wait_ge(s_xd, XD_MEGA + 1)
                        for p in range(8):
                            t_.wait_ge(s_2d, d2(t, dqp(p)))
                            li = t_.matmul(P2[0:16, :], kbd[:, p * 16:(p + 1) * 16],
                                      qp_b[:, p * 512:(p + 1) * 512],
                                      start=(p == 0), stop=(p == 7))
                        li.then_inc(s_2p, 1)
                        for c in range(8):
                            if c < 2:
                                t_.wait_ge(s_2d, d2(t, dqp(6 + c)))
                            else:
                                t_.wait_ge(s_2d, d2(t, ddiv(c - 2)))
                            t_.matmul(PQ[c % 2][:, :], kv2[:, c * 128:(c + 1) * 128],
                                      qp_b[:, c * 512:(c + 1) * 512], start=True, stop=True).then_inc(s_2p, 1)
                            t_.wait_ge(s_2d, d2(t, D_RN2))
                            if c >= 2:
                                t_.wait_ge(s_2a, a2(t, aRnb(c - 2)))
                            t_.matmul(PR[c % 2][:, :], sel16[:, c * 128:(c + 1) * 128],
                                      rn_b[:, :], start=True, stop=True).then_inc(s_2p, 1)
                        def asm_base(tt):
                            j = 4 * t + tt
                            asmL, asmR = asm_banks(tt)
                            prj = PPRJ[tt % 2]
                            if j >= 2:
                                t2_, tt2 = (j - 2) // 4, (j - 2) % 4
                                t_.wait_ge(s_2d, d2(t2_, dg(tt2, 18)))
                            if tt % 2 == 0:
                                t_.wait_ge(s_2d, d2(t, ddiv(7)))
                            if tt == 0:
                                t_.wait_ge(s_2d, d2(t, D_RN2))
                                if t > 0:
                                    t_.wait_ge(s_2d, d2(t - 1, dg(2, 15)))
                            elif tt == 1:
                                if t > 0:
                                    t_.wait_ge(s_2d, d2(t - 1, dg(3, 15)))
                            else:
                                t_.wait_ge(s_2d, d2(t, dg(tt - 2, 15)))
                            for kc in range(8):
                                t_.matmul(asmL, xn[:, kc * TPC + j * 128: kc * TPC + (j + 1) * 128],
                                          wk[:, kc * 1024: kc * 1024 + 512], start=(kc == 0), stop=False)
                                t_.matmul(asmR, xn[:, kc * TPC + j * 128: kc * TPC + (j + 1) * 128],
                                          wk[:, kc * 1024 + 512: (kc + 1) * 1024], start=(kc == 0), stop=False)
                                t_.matmul(prj[:, 0:WPD], xn[:, kc * TPC + j * 128: kc * TPC + (j + 1) * 128],
                                          wpd[:, kc * WPD: (kc + 1) * WPD], start=(kc == 0), stop=False)
                            t_.matmul(prj[:, 0:WPD], onesr_b[:, :], bp_row[:, :], start=False, stop=True).then_inc(s_2p, 1)
                            t_.matmul(asmL, onesr_b[:, :], bo_row[0:1, 0:512], start=False, stop=False)
                            t_.matmul(asmR, onesr_b[:, :], bo_row[0:1, 512:1024], start=False, stop=False)
                            for kc in range(8):
                                t_.wait_ge(s_2d, d2(t, ddiv(kc)))
                                t_.matmul(asmL, o_b[:, kc * 512 + tt * 128: kc * 512 + (tt + 1) * 128],
                                          wo[:, kc * 1024: kc * 1024 + 512], start=False, stop=False)
                                t_.matmul(asmR, o_b[:, kc * 512 + tt * 128: kc * 512 + (tt + 1) * 128],
                                          wo[:, kc * 1024 + 512: (kc + 1) * 1024], start=False, stop=False)

                        def asm_close(tt):
                            asmL, asmR = asm_banks(tt)
                            t_.wait_ge(s_2d, d2(t, dg(tt, 15)))
                            if tt == 0:
                                t_.wait_ge(s_2a, a2(t, aRnb(7)))
                            t_.transpose(P3[:, :].bitcast(BF16)[0:16, 0:128], t1b[:, :], identb[:, :]).then_inc(s_2p, 1)
                            t_.wait_ge(s_2d, d2(t, dg(tt, 16)))
                            t_.matmul(asmL, tpt[:, :], wut[:, 0:512], start=False, stop=True)
                            t_.matmul(asmR, tpt[:, :], wut[:, 512:1024], start=False, stop=True).then_inc(s_2p, 1)

                        asm_base(0)
                        asm_base(1)
                        asm_close(0)
                        asm_base(2)
                        asm_close(1)
                        asm_base(3)
                        asm_close(2)
                        asm_close(3)

                @blk.vector
                def _(v):
                    for t in range(nt4):
                        for c in range(8):
                            v.wait_ge(s_2p, p2(t, pQ(c)))
                            v.tensor_scalar(kq[:, c * 512:(c + 1) * 512], PQ[c % 2][:, :],
                                            bqs[:, c:c + 1], None, ALU.add).then_inc(s_2d, 1)
                        for p in range(8):
                            v.wait_ge(s_2p, p2(t, pqp(p)))
                            v.tensor_scalar(qp_b[:, p * 512:(p + 1) * 512], PQ[p % 2][:, :],
                                            0.0, EPS_AT, ALU.max, ALU.add).then_inc(s_2d, 1)
                        if t == 0:
                            v.wait_ge(s_xg, XG_MEGA + 16)
                            for p in range(8):
                                v.tensor_copy(kv2[0:64, p * 128: p * 128 + 64], kvpart[0:64, p * 128: p * 128 + 64])
                                v.tensor_copy(kv2[64:128, p * 128 + 64: p * 128 + 128], kvpart[64:128, p * 128 + 64: p * 128 + 128])
                                v.tensor_copy(kbd[0:64, p * 16 + 2 * p: p * 16 + 2 * p + 1], kvpart[0:64, 1024 + p: 1025 + p])
                                lastc = v.tensor_copy(kbd[64:128, p * 16 + 2 * p + 1: p * 16 + 2 * p + 2], kvpart[64:128, 1024 + p: 1025 + p])
                            lastc.then_inc(s_xd, 1)
                        v.wait_ge(s_2p, p2(t, P_NORM))
                        v.tensor_scalar(norm_s[:, :], P2[0:16, :], EPS_AT, None, ALU.add).then_inc(s_2d, 1)
                        v.wait_ge(s_2d, d2(t, D_RN1))
                        with nc.allow_low_precision(reason="rn feeds a bf16 matmul operand"):
                            v.reciprocal(rn_b[:, :], norm_s[:, :]).then_inc(s_2d, 1)
                        for c in range(8):
                            v.wait_ge(s_2p, p2(t, pqkv(c)))
                            v.wait_ge(s_2a, a2(t, aRnb(c)))
                            v.tensor_tensor(o_b[:, c * 512:(c + 1) * 512], PQ[c % 2][:, :],
                                            rnb_sb[c % 2][:, :], ALU.mult).then_inc(s_2d, 1)
                        for tt in range(4):
                            j = 4 * t + tt
                            prj = PPRJ[tt % 2]
                            asmL, asmR = asm_banks(tt)
                            v.wait_ge(s_2a, a2(t, ag(tt, 1)))
                            v.tensor_scalar(m1[:, :], ssq[:, :], 1e-24, None, ALU.max).then_inc(s_2d, 1)
                            v.wait_ge(s_2a, a2(t, ag(tt, 3)))
                            v.tensor_scalar(rns[:, :], rnorm[:, :], sclbc[:, 0:1], None, ALU.mult).then_inc(s_2d, 1)
                            v.tensor_tensor(lg1[:, :], prj[:, 271:275], rn4bc[:, :], ALU.mult).then_inc(s_2d, 1)
                            v.wait_ge(s_2d, d2(t, dg(tt, 3)))
                            v.tensor_scalar(logits[:, :], lg1[:, :], rns[:, 0:1], None, ALU.mult).then_inc(s_2d, 1)
                            v.wait_ge(s_2d, d2(t, dg(tt, 4)))
                            v.reduce_max(nmx[:, :], logits[:, :], axis=AX, negate=True).then_inc(s_2d, 1)
                            v.wait_ge(s_2a, a2(t, ag(tt, 4)))
                            v.reduce_sum(sev[:, :], exps[:, :], axis=AX).then_inc(s_2d, 1)
                            v.wait_ge(s_2d, d2(t, dg(tt, 6)))
                            v.reciprocal(rse[:, :], sev[:, :]).then_inc(s_2d, 1)
                            v.wait_ge(s_2d, d2(t, dg(tt, 7)))
                            v.tensor_scalar(lw[:, :], exps[:, :], rse[:, 0:1], None, ALU.mult).then_inc(s_2d, 1)
                            v.wait_ge(s_2d, d2(t, dg(tt, 8)))
                            v.reduce_max(mxv[:, :], lw[:, :], axis=AX).then_inc(s_2d, 1)
                            v.wait_ge(s_2d, d2(t, dg(tt, 9)))
                            v.tensor_scalar(maskv[:, :], lw[:, :], mxv[:, 0:1], None, ALU.is_ge).then_inc(s_2d, 1)
                            v.wait_ge(s_2d, d2(t, dg(tt, 10)))
                            v.tensor_mul(retv[:, :], lw[:, :], maskv[:, :]).then_inc(s_2d, 1)
                            v.wait_ge(s_2d, d2(t, dg(tt, 11)))
                            if j > 0:
                                t3, tt3 = (j - 1) // 4, (j - 1) % 4
                                v.wait_ge(s_2p, p2(t3, ptpt(tt3)))
                            off = 0
                            for ei, r in enumerate([1, 2, 4, 8]):
                                v.tensor_tensor(t1b[:, off:off + r], prj[:, 256 + off:256 + off + r],
                                                retv[:, ei:ei + 1].broadcast_to([128, r]), ALU.mult).then_inc(s_2d, 1)
                                off += r
                            v.wait_ge(s_2p, p2(t, ptpt(tt)))
                            v.tensor_copy(tpt[:, :], P3[:, :].bitcast(BF16)[0:16, 0:128]).then_inc(s_2d, 1)
                            v.wait_ge(s_2p, p2(t, pasm(tt)))
                            v.wait_ge(s_2x, 16 * (j + 1))
                            if j >= 2:
                                v.wait_ge(s_2s, 16 * (j - 1))
                            if j == 0:
                                v.wait_ge(s_xg, XG_MEGA + 16)
                            if j == 1:
                                v.wait_ge(s_xd, XD_MEGA + 1)
                            v.tensor_tensor(out_tm[j % 2][:, 0:512], asmL, xtm[j % 3][:, 0:512], ALU.add).then_inc(s_2d, 1)
                            v.tensor_tensor(out_tm[j % 2][:, 512:1024], asmR, xtm[j % 3][:, 512:1024], ALU.add).then_inc(s_2d, 1)

                @blk.scalar
                def _(s):
                    for t in range(nt4):
                        for c in range(8):
                            s.wait_ge(s_2p, p2(t, prnb(c)))
                            if c >= 2:
                                s.wait_ge(s_2d, d2(t, ddiv(c - 2)))
                            s.activation(rnb_sb[c % 2][:, :], PR[c % 2][:, :], AF.Copy).then_inc(s_2a, 1)
                        for tt in range(4):
                            j = 4 * t + tt
                            prj = PPRJ[tt % 2]
                            s.wait_ge(s_2p, p2(t, pprj(tt)))
                            if j > 0:
                                t3, tt3 = (j - 1) // 4, (j - 1) % 4
                                s.wait_ge(s_2d, d2(t3, dg(tt3, 1)))
                            s.activation(sqg[:, :], prj[:, 0:256], AF.Square, accum_out=ssq[:, :]).then_inc(s_2a, 1)
                            s.wait_ge(s_2d, d2(t, dg(tt, 1)))
                            s.activation(lnp[:, :], m1[:, :], AF.Ln).then_inc(s_2a, 1)
                            s.wait_ge(s_2a, a2(t, ag(tt, 2)))
                            if j > 0:
                                t3, tt3 = (j - 1) // 4, (j - 1) % 4
                                s.wait_ge(s_2d, d2(t3, dg(tt3, 2)))
                            s.activation(rnorm[:, :], lnp[:, :], AF.Exp, scale=-0.5).then_inc(s_2a, 1)
                            s.wait_ge(s_2d, d2(t, dg(tt, 5)))
                            if j > 0:
                                t3, tt3 = (j - 1) // 4, (j - 1) % 4
                                s.wait_ge(s_2d, d2(t3, dg(tt3, 8)))
                            s.activation(exps[:, :], logits[:, :], AF.Exp, bias=nmx[:, 0:1]).then_inc(s_2a, 1)

                @blk.sync
                def _(sy):
                    for j in range(4 * nt4):
                        if j >= 3:
                            t3, tt3 = (j - 3) // 4, (j - 3) % 4
                            sy.wait_ge(s_2d, d2(t3, dg(tt3, 18)))
                        sy.dma_start(out=xtm[j % 3][:, :], in_=xh[j * 128:(j + 1) * 128, :]).then_inc(s_2x, 16)

                @blk.gpsimd
                def _(g):
                    g.wait_ge(s_cc, 1)
                    g.dma_start(out=kvpart[:, :], in_=ccout[:, :]).then_inc(s_xg, 16)
                    for j in range(4 * nt4):
                        t3, tt3 = j // 4, j % 4
                        g.wait_ge(s_2d, d2(t3, dg(tt3, 18)))
                        g.dma_start(out=yh[j * 128:(j + 1) * 128, :], in_=out_tm[j % 2][:, 0:1024]).then_inc(s_2s, 16)
                    g.wait_ge(s_2s, 16 * 4 * nt4)

    return nc


def _consts():
    c = {}
    c["identb"] = np.eye(128, dtype=ml_dtypes.bfloat16)
    c["identf"] = np.eye(128, dtype=np.float32)
    c["onesr_b"] = np.ones((1, 128), dtype=ml_dtypes.bfloat16)
    c["onesr_f"] = np.ones((1, 128), dtype=np.float32)
    c["onesc_b"] = np.ones((128, 1), dtype=ml_dtypes.bfloat16)
    c["onesc_f"] = np.ones((128, 1), dtype=np.float32)
    sel = np.zeros((16, 1024), dtype=ml_dtypes.bfloat16)
    for ch in range(8):
        for jj in range(128):
            h = 2 * ch + (1 if jj >= 64 else 0)
            sel[h, ch * 128 + jj] = 1
    c["sel16"] = sel
    return c


def kernel(**inputs):
    x = np.ascontiguousarray(np.asarray(inputs["L0_attn_x"], dtype=np.float32))
    wd = np.ascontiguousarray(np.concatenate(
        [np.asarray(inputs[f"L0_ffn_Wd{i}"], dtype=np.float32) for i in range(4)], axis=0))
    wu = np.ascontiguousarray(np.concatenate(
        [np.asarray(inputs[f"L0_ffn_Wu{i}"], dtype=np.float32) for i in range(4)], axis=1))
    common = dict(
        wq=np.ascontiguousarray(inputs["L0_attn_Wq"]),
        wk=np.ascontiguousarray(inputs["L0_attn_Wk"]),
        wv=np.ascontiguousarray(inputs["L0_attn_Wv"]),
        wo=np.ascontiguousarray(inputs["L0_attn_Wo"]),
        bq=np.ascontiguousarray(inputs["L0_attn_bq"]),
        bk=np.ascontiguousarray(inputs["L0_attn_bk"]),
        bv=np.ascontiguousarray(inputs["L0_attn_bv"]),
        bo=np.ascontiguousarray(inputs["L0_attn_bo"]),
        rw=np.ascontiguousarray(inputs["L0_attn_rw"]),
        w=np.ascontiguousarray(inputs["L0_ffn_W"]),
        wp=np.ascontiguousarray(inputs["L0_ffn_Wp"]),
        bp=np.ascontiguousarray(inputs["L0_ffn_bp"]),
        sim=np.ascontiguousarray(inputs["L0_ffn_sim"]),
        temp=np.ascontiguousarray(inputs["L0_ffn_temp"]),
        wd=wd, wu=wu,
    )
    common.update(_consts())

    if "nc" not in _CACHE:
        _CACHE["nc"] = build_nc()
    nc = _CACHE["nc"]

    in_maps = []
    for c in range(8):
        b, h = c // 2, c % 2
        m = dict(common)
        m["x"] = np.ascontiguousarray(x[b, h * TPC:(h + 1) * TPC, :])
        in_maps.append(m)

    res = run_bass_kernel_spmd(nc, in_maps, core_ids=list(range(8)))
    out = np.empty((4, 8192, D), dtype=np.float32)
    for c in range(8):
        b, h = c // 2, c % 2
        out[b, h * TPC:(h + 1) * TPC, :] = res.results[c]["y"]
    return out


# version 10
# speedup vs baseline: 1.0372x; 1.0372x over previous
import numpy as np
import ml_dtypes
from contextlib import ExitStack

import concourse.bass as bass
import concourse.mybir as mybir
from concourse.bass_utils import run_bass_kernel_spmd

F32 = mybir.dt.float32
BF16 = mybir.dt.bfloat16
AF = mybir.ActivationFunctionType
ALU = mybir.AluOpType
AX = mybir.AxisListType.X

D = 1024
TPC = 4096
NCH = 32
NT = 8
EPS_LN = 1e-5
EPS_AT = 1e-6
CLAMP_MAX = float(np.log(100.0))
WPD = 276

_CACHE = {}
_DBG = {}


def build_nc():
    nc = bass.Bass("TRN2", target_bir_lowering=False, detect_race_conditions=False)

    xh = nc.dram_tensor("x", [TPC, D], F32, kind="ExternalInput")
    wqh = nc.dram_tensor("wq", [D, D], F32, kind="ExternalInput")
    wkh = nc.dram_tensor("wk", [D, D], F32, kind="ExternalInput")
    wvh = nc.dram_tensor("wv", [D, D], F32, kind="ExternalInput")
    woh = nc.dram_tensor("wo", [D, D], F32, kind="ExternalInput")
    bqh = nc.dram_tensor("bq", [D], F32, kind="ExternalInput")
    bkh = nc.dram_tensor("bk", [D], F32, kind="ExternalInput")
    bvh = nc.dram_tensor("bv", [D], F32, kind="ExternalInput")
    boh = nc.dram_tensor("bo", [D], F32, kind="ExternalInput")
    rwh = nc.dram_tensor("rw", [16, 64, 64], F32, kind="ExternalInput")
    wh = nc.dram_tensor("w", [D, D], F32, kind="ExternalInput")
    wph = nc.dram_tensor("wp", [256, D], F32, kind="ExternalInput")
    bph = nc.dram_tensor("bp", [256], F32, kind="ExternalInput")
    simh = nc.dram_tensor("sim", [256, 4], F32, kind="ExternalInput")
    temph = nc.dram_tensor("temp", [1], F32, kind="ExternalInput")
    wdh = nc.dram_tensor("wd", [15, D], F32, kind="ExternalInput")
    wuh = nc.dram_tensor("wu", [D, 15], F32, kind="ExternalInput")
    identb_h = nc.dram_tensor("identb", [128, 128], BF16, kind="ExternalInput")
    identf_h = nc.dram_tensor("identf", [128, 128], F32, kind="ExternalInput")
    onesr_b_h = nc.dram_tensor("onesr_b", [1, 128], BF16, kind="ExternalInput")
    onesr_f_h = nc.dram_tensor("onesr_f", [1, 128], F32, kind="ExternalInput")
    onesc_b_h = nc.dram_tensor("onesc_b", [128, 1], BF16, kind="ExternalInput")
    onesc_f_h = nc.dram_tensor("onesc_f", [128, 1], F32, kind="ExternalInput")
    sel16_h = nc.dram_tensor("sel16", [16, 1024], BF16, kind="ExternalInput")
    yh = nc.dram_tensor("y", [TPC, D], F32, kind="ExternalOutput")

    ccin = nc.dram_tensor("ccin", [128, 1032], F32, kind="Internal")
    ccout = nc.dram_tensor("ccout", [128, 1032], F32, kind="Internal")

    with ExitStack() as ctx:
        e = ctx.enter_context

        xn = e(nc.sbuf_tensor("xn", [128, 8 * TPC], BF16))
        wq = e(nc.sbuf_tensor("wq_b", [128, 8 * 1024], BF16))
        wk = e(nc.sbuf_tensor("wk_b", [128, 8 * 1024], BF16))
        wv = e(nc.sbuf_tensor("wv_b", [128, 8 * 1024], BF16))
        wo = e(nc.sbuf_tensor("wo_b", [128, 8 * 1024], BF16))
        wpd = e(nc.sbuf_tensor("wpd", [128, 8 * WPD], BF16))
        wut = e(nc.sbuf_tensor("wut", [16, 1024], BF16))
        wu_st = e(nc.sbuf_tensor("wu_st", [128, 120], BF16))
        rw2 = e(nc.sbuf_tensor("rw2", [128, 1024], BF16))
        sel16 = e(nc.sbuf_tensor("sel16_s", [16, 1024], BF16))
        kv2 = e(nc.sbuf_tensor("kv2", [128, 1024], BF16))
        kbd = e(nc.sbuf_tensor("kbd", [128, 128], BF16))
        identb = e(nc.sbuf_tensor("identb_s", [128, 128], BF16))
        identf = e(nc.sbuf_tensor("identf_s", [128, 128], F32))
        onesr_b = e(nc.sbuf_tensor("onesr_b_s", [1, 128], BF16))
        onesr_f = e(nc.sbuf_tensor("onesr_f_s", [1, 128], F32))
        onesc_b = e(nc.sbuf_tensor("onesc_b_s", [128, 1], BF16))
        onesc_f = e(nc.sbuf_tensor("onesc_f_s", [128, 1], F32))
        bqs = e(nc.sbuf_tensor("bqs", [128, 8], F32))
        bks = e(nc.sbuf_tensor("bks", [128, 8], F32))
        bvbc = e(nc.sbuf_tensor("bvbc", [128, 1024], F32))
        bo_row = e(nc.sbuf_tensor("bo_row", [1, 1024], BF16))
        bp_row = e(nc.sbuf_tensor("bp_row", [1, WPD], BF16))
        rn4bc = e(nc.sbuf_tensor("rn4bc", [128, 4], F32))
        sclbc = e(nc.sbuf_tensor("sclbc", [128, 1], F32))
        cst = e(nc.sbuf_tensor("cst", [128, 2], F32))
        xtm = [e(nc.sbuf_tensor(f"xtm{i}", [128, 1024], F32)) for i in range(3)]
        sumx = e(nc.sbuf_tensor("sumx", [128, 2], F32))
        sumsq = e(nc.sbuf_tensor("sumsq", [128, 2], F32))
        meanv = e(nc.sbuf_tensor("meanv", [128, 2], F32))
        msq = e(nc.sbuf_tensor("msq", [128, 2], F32))
        varv = e(nc.sbuf_tensor("varv", [128, 2], F32))
        lnt = e(nc.sbuf_tensor("lnt", [128, 2], F32))
        rstd = e(nc.sbuf_tensor("rstd", [128, 2], F32))
        kq = e(nc.sbuf_tensor("kq", [128, 8 * 512], BF16))
        kpbuf = e(nc.sbuf_tensor("kpbuf", [128, 1024], BF16))
        kp_b = [kpbuf[:, i * 128:(i + 1) * 128] for i in range(8)]
        v_b = e(nc.sbuf_tensor("v_b", [128, 1024], BF16))
        qp_b = e(nc.sbuf_tensor("qp_b", [128, 8 * 512], BF16))
        o_b = e(nc.sbuf_tensor("o_b", [128, 8 * 512], BF16))
        norm_s = e(nc.sbuf_tensor("norm_s", [16, 512], F32))
        rn_b = e(nc.sbuf_tensor("rn_b", [16, 512], BF16))
        rnb_sb = [e(nc.sbuf_tensor(f"rnb{i}", [128, 512], F32)) for i in range(2)]
        out_tm = [e(nc.sbuf_tensor(f"outtm{i}", [128, 1040], F32)) for i in range(2)]
        xnt = [o_b[0:128, 2048:3072], o_b[0:128, 3072:4096]]
        sqsc = o_b[0:128, 0:1024]
        wd_bf = o_b[0:15, 1024:2048]
        kvstage = out_tm[0][0:128, 0:1032]
        kvpart = out_tm[1][0:128, 0:1032]
        bv_row = out_tm[0][0:1, 0:1024]
        m1 = e(nc.sbuf_tensor("m1", [128, 1], F32))
        lnp = e(nc.sbuf_tensor("lnp", [128, 1], F32))
        rnorm = e(nc.sbuf_tensor("rnorm", [128, 1], F32))
        rns = e(nc.sbuf_tensor("rns", [128, 1], F32))
        lg1 = e(nc.sbuf_tensor("lg1", [128, 4], F32))
        logits = e(nc.sbuf_tensor("logits", [128, 4], F32))
        nmx = e(nc.sbuf_tensor("nmx", [128, 1], F32))
        exps = e(nc.sbuf_tensor("exps", [128, 4], F32))
        sev = e(nc.sbuf_tensor("sev", [128, 1], F32))
        rse = e(nc.sbuf_tensor("rse", [128, 1], F32))
        lw = e(nc.sbuf_tensor("lw", [128, 4], F32))
        mxv = e(nc.sbuf_tensor("mxv", [128, 1], F32))
        maskv = e(nc.sbuf_tensor("maskv", [128, 4], F32))
        retv = e(nc.sbuf_tensor("retv", [128, 4], F32))
        sqg = e(nc.sbuf_tensor("sqg", [128, 256], BF16))
        ssq = e(nc.sbuf_tensor("ssq", [128, 1], F32))
        t1b = e(nc.sbuf_tensor("t1b", [128, 16], BF16))
        tpt = e(nc.sbuf_tensor("tpt", [16, 128], BF16))
        bq_st = e(nc.sbuf_tensor("bq_st", [8, 128], F32))
        bk_st = e(nc.sbuf_tensor("bk_st", [8, 128], F32))
        bp2 = e(nc.sbuf_tensor("bp2", [2, 128], F32))
        bp2t = e(nc.sbuf_tensor("bp2t", [128, 2], F32))
        sim_f = e(nc.sbuf_tensor("sim_f", [128, 8], F32))
        sim_b = e(nc.sbuf_tensor("sim_b", [128, 8], BF16))
        sim_sq = e(nc.sbuf_tensor("sim_sq", [128, 8], F32))
        s4 = e(nc.sbuf_tensor("s4", [1, 4], F32))
        s4m = e(nc.sbuf_tensor("s4m", [1, 4], F32))
        tmp4 = e(nc.sbuf_tensor("tmp4", [1, 4], F32))
        rn4 = e(nc.sbuf_tensor("rn4", [1, 4], F32))
        scl = e(nc.sbuf_tensor("scl", [1, 1], F32))
        sclm = e(nc.sbuf_tensor("sclm", [1, 1], F32))
        sclE = e(nc.sbuf_tensor("sclE", [1, 1], F32))

        SEMN = ["s_pl", "s_ax", "s_ag", "s_ad", "s_aa", "s_ap", "s_1p", "s_1d",
                "s_cst", "s_xd", "s_xg", "s_cc",
                "s_2p", "s_2d", "s_2a", "s_2x", "s_2s"]
        sem = {n: e(nc.semaphore(n)) for n in SEMN}

        xn_r = xn[:, :].rearrange("p (c t) -> p c t", c=8)

        with nc.psum_tensor("pa0", [128, 512], F32) as pa0, \
             nc.psum_tensor("p1k0", [128, 512], F32) as p1k0, \
             nc.psum_tensor("p1k1", [128, 512], F32) as p1k1, \
             nc.psum_tensor("p1v", [128, 1024], F32) as p1v, \
             nc.psum_tensor("kvA", [128, 512], F32) as kvA, \
             nc.psum_tensor("kvB", [128, 512], F32) as kvB, \
             nc.psum_tensor("kvS", [128, 512], F32) as kvS:
            B1, B2 = p1k0, p1k1
            p1k = [p1k0, p1k1]
            pab = pa0[:, :].bitcast(BF16)
            pab_r = pab.rearrange("p (c t) -> p c t", c=8)
            B1b = B1[:, :].bitcast(BF16)
            B2b = B2[:, :].bitcast(BF16)

            C = {n: {"n": 0, "m": {}} for n in SEMN}
            streams = {k: [] for k in ["sync", "gpsimd", "tensor", "vector", "scalar"]}

            def op(eng, fn, waits=(), inc=None, by=1):
                streams[eng].append((fn, list(waits), inc, by))

            def cdma(dst, src):
                return lambda g: g.dma_start(out=dst, in_=src)

            consts = [
                ("identb", identb[:, :], identb_h[:, :]),
                ("identf", identf[:, :], identf_h[:, :]),
                ("onesr_b", onesr_b[:, :], onesr_b_h[:, :]),
                ("onesr_f", onesr_f[:, :], onesr_f_h[:, :]),
                ("onesc_b", onesc_b[:, :], onesc_b_h[:, :]),
                ("onesc_f", onesc_f[:, :], onesc_f_h[:, :]),
                ("sel16", sel16[:, :], sel16_h[:, :]),
                ("bq_st", bq_st[:, :], bqh.ap().rearrange("(c j) -> c j", c=8)),
                ("bk_st", bk_st[:, :], bkh.ap().rearrange("(c j) -> c j", c=8)),
                ("bp2", bp2[:, :], bph.ap().rearrange("(c j) -> c j", c=2)),
                ("bv_row", bv_row[:, :], bvh.ap().rearrange("(a n) -> a n", a=1)),
                ("sim0", sim_f[:, 0:4], simh[0:128, :]),
                ("sim1", sim_f[:, 4:8], simh[128:256, :]),
                ("scl", scl[:, :], temph.ap().rearrange("(a n) -> a n", a=1)),
            ]
            for nm, d, s_ in consts:
                op("sync", cdma(d, s_), inc=("s_pl", ("c", nm)), by=16)
            for i in range(NCH):
                w = []
                if i >= 3:
                    w = [("s_ad", ("app", i - 3)), ("s_aa", ("sq", i - 3))]
                op("sync", cdma(xtm[i % 3][:, :], xh[i * 128:(i + 1) * 128, :]),
                   waits=w, inc=("s_ax", ("x", i)), by=16)

            def ms(dst, val):
                return lambda g: g.memset(dst, val)

            op("gpsimd", ms(cst[:, 0:1], EPS_LN))
            op("gpsimd", ms(cst[:, 1:2], 1e-24), inc=("s_cst", ("cst1",)))
            for d, v in [(rw2[:, :], 0.0), (kv2[:, :], 0.0), (kbd[:, :], 0.0),
                         (t1b[:, 15:16], 0.0), (wpd[:, :], 0.0),
                         (bp_row[:, :], 0.0)]:
                op("gpsimd", ms(d, v))
            op("gpsimd", ms(wut[:, :], 0.0), inc=("s_cst", ("msets",)))
            op("gpsimd", cdma(bp_row[0:1, 0:256], bph.ap().rearrange("(a n) -> a n", a=1)),
               inc=("s_ag", ("bpr",)), by=16)
            for oc in range(8):
                op("gpsimd", cdma(wu_st[:, oc * 15:(oc + 1) * 15],
                                  wuh[oc * 128:(oc + 1) * 128, :]),
                   inc=("s_ag", ("wu", oc)), by=16)
            for pc in range(2):
                op("gpsimd", cdma(qp_b[:, pc * 1024:(pc + 1) * 1024],
                                  wph[pc * 128:(pc + 1) * 128, :]),
                   inc=("s_ag", ("wp", pc)), by=16)
            op("gpsimd", cdma(bo_row[:, :], boh.ap().rearrange("(a n) -> a n", a=1)),
               inc=("s_ag", ("bo",)), by=16)
            op("gpsimd", cdma(wd_bf[:, :], wdh[:, :]), inc=("s_ag", ("wd",)), by=16)
            for nm, hh in [("wk", wkh), ("wv", wvh)]:
                dst = {"wk": wk, "wv": wv}[nm]
                for c in range(8):
                    op("gpsimd", cdma(dst[:, c * 1024:(c + 1) * 1024],
                                      hh[c * 128:(c + 1) * 128, :]),
                       inc=("s_ag", (nm, c)), by=16)
            for p in range(8):
                op("gpsimd", cdma(rw2[0:64, p * 128:p * 128 + 64], rwh[2 * p]),
                   inc=("s_ag", ("rw", 2 * p)), by=16)
                op("gpsimd", cdma(rw2[64:128, p * 128 + 64:p * 128 + 128], rwh[2 * p + 1]),
                   inc=("s_ag", ("rw", 2 * p + 1)), by=16)
            for nm, hh in [("wq", wqh), ("wo", woh)]:
                dst = {"wq": wq, "wo": wo}[nm]
                for c in range(8):
                    op("gpsimd", cdma(dst[:, c * 1024:(c + 1) * 1024],
                                      hh[c * 128:(c + 1) * 128, :]),
                       inc=("s_ag", (nm, c)), by=16)
            for c in range(8):
                w = [("s_ap", ("wps",))] if c < 2 else [("s_1p", ("wt", c - 2))]
                op("gpsimd", cdma(qp_b[:, (c % 2) * 1024:(c % 2 + 1) * 1024],
                                  wh[c * 128:(c + 1) * 128, :]),
                   waits=w, inc=("s_ag", ("W", c)), by=16)
            op("gpsimd", cdma(ccin[:, :], kvstage[:, :]),
               waits=[("s_xd", ("kvst",))], inc=("s_xg", ("ccin",)), by=16)

            def coll(g):
                g.collective_compute(
                    "AllReduce", ALU.add,
                    replica_groups=[[0, 1], [2, 3], [4, 5], [6, 7]],
                    ins=[ccin.ap().opt()], outs=[ccout.ap().opt()],
                ).then_inc(sem["s_cc"])
                return None
            op("gpsimd", coll, waits=[("s_xg", ("ccin",))])

            def ln_grp(i):
                par = slice(i % 2, i % 2 + 1)

                def red(v, i=i, par=par):
                    return v.reduce_sum(sumx[:, par], xtm[i % 3][:, :], axis=AX)
                op("vector", red, waits=[("s_ax", ("x", i))], inc=("s_ad", ("red", i)))

                def mean(v, i=i, par=par):
                    return v.tensor_scalar(meanv[:, par], sumx[:, par], 1.0 / D, None, ALU.mult)
                op("vector", mean, waits=[("s_ad", ("red", i))], inc=("s_ad", ("mean", i)))

                def msqf(v, i=i, par=par):
                    return v.tensor_mul(msq[:, par], meanv[:, par], meanv[:, par])
                op("vector", msqf, waits=[("s_ad", ("mean", i))], inc=("s_ad", ("msq", i)))

                def varf(v, i=i, par=par):
                    return v.scalar_tensor_tensor(varv[:, par], sumsq[:, par], 1.0 / D,
                                                  msq[:, par], ALU.mult, ALU.subtract)
                op("vector", varf, waits=[("s_ad", ("msq", i)), ("s_aa", ("sq", i))],
                   inc=("s_ad", ("var", i)))
                if i >= 1:
                    j = i - 1
                    parj = slice(j % 2, j % 2 + 1)

                    def app(v, j=j, parj=parj):
                        return v.tensor_scalar(xnt[j % 2][:, :], xtm[j % 3][:, :],
                                               meanv[:, parj], rstd[:, parj],
                                               ALU.subtract, ALU.mult)
                    w = [("s_aa", ("rstd", j))]
                    if j >= 2:
                        w.append(("s_ap", ("T", j - 2)))
                    op("vector", app, waits=w, inc=("s_ad", ("app", j)))

            def vcopy(dst, src, waits, inc):
                op("vector", (lambda v, d=dst, s_=src: v.tensor_copy(d, s_)),
                   waits=waits, inc=inc)

            def prep_d(stage):
                if stage == 1:
                    vcopy(bvbc[:, 0:512], B1[:, :], [("s_ap", ("bvmm",))], ("s_ad", ("bvbc0",)))
                    vcopy(bvbc[:, 512:1024], B2[:, :], [], ("s_ad", ("bvbc",)))
                elif stage == 2:
                    for oc in range(4):
                        vcopy(wut[0:15, oc * 128:(oc + 1) * 128],
                              B2b[0:15, oc * 128:(oc + 1) * 128],
                              [("s_ap", ("wutT", oc)), ("s_cst", ("msets",))],
                              ("s_ad", ("wutev", oc)))
                elif stage == 22:
                    for oc in range(4, 8):
                        vcopy(wut[0:15, oc * 128:(oc + 1) * 128],
                              B2b[0:15, oc * 128:(oc + 1) * 128],
                              [("s_ap", ("wutT", oc))], ("s_ad", ("wutev", oc)))
                elif stage == 3:
                    for pc in range(2):
                        src = (B1b if pc == 0 else B2b)
                        wt = [("s_ap", ("wpt0",)), ("s_cst", ("msets",))] if pc == 0 \
                            else [("s_ap", ("wpt1",))]
                        for dc in range(8):
                            vcopy(wpd[:, dc * WPD + pc * 128: dc * WPD + (pc + 1) * 128],
                                  src[:, dc * 128:(dc + 1) * 128],
                                  wt if dc == 0 else [],
                                  ("s_ad", (f"wpt{pc}ev", dc)))
                    vcopy(sim_b[:, :], sim_f[:, :], [("s_pl", ("c", "sim1"))], ("s_ad", ("simc",)))

                    def simsq(v):
                        return v.tensor_mul(sim_sq[:, :], sim_f[:, :], sim_f[:, :])
                    op("vector", simsq, waits=[], inc=("s_ad", ("simsq",)))
                elif stage == 4:
                    for dc in range(8):
                        vcopy(wpd[:, dc * WPD + 256: dc * WPD + 271],
                              B1b[:, dc * 16: dc * 16 + 15],
                              [("s_ap", ("wdt",))] if dc == 0 else [],
                              ("s_ad", ("wdtev", dc)))
                    vcopy(bqs[:, :], B2[:, 0:8], [], ("s_ad", ("bqsev",)))
                    vcopy(bks[:, :], B2[:, 8:16], [], ("s_ad", ("bksev",)))
                    vcopy(bp2t[:, :], B2[:, 16:18], [], ("s_ad", ("bp2tev",)))
                elif stage == 5:
                    for dc in range(8):
                        vcopy(wpd[:, dc * WPD + 271: dc * WPD + 275],
                              B1[:, dc * 4:(dc + 1) * 4],
                              [("s_ap", ("wps",))] if dc == 0 else [],
                              ("s_ad", ("wpsev", dc)))
                    vcopy(bp_row[:, 271:275], B2[0:1, 0:4], [], ("s_ad", ("bprow1",)))
                    vcopy(s4[:, :], B2[0:1, 8:12], [], ("s_ad", ("s4copy",)))

                    def sclmf(v):
                        return v.tensor_scalar(sclm[:, :], scl[:, :], CLAMP_MAX, None, ALU.min)
                    op("vector", sclmf, waits=[("s_pl", ("c", "scl"))], inc=("s_ad", ("sclm",)))

                    def s4mf(v):
                        return v.tensor_scalar(s4m[:, :], s4[:, :], 1e-24, None, ALU.max)
                    op("vector", s4mf, waits=[("s_ad", ("s4copy",))], inc=("s_ad", ("s4m",)))
                elif stage == 6:
                    vcopy(rn4bc[:, :], B1[:, 0:4], [("s_ap", ("rnscl",))], ("s_ad", ("rn4ev",)))
                    vcopy(sclbc[:, :], B1[:, 8:9], [], ("s_ad", ("prepdone",)))

            def p3_s(t, s):
                for h in range(2):
                    w = [("s_1p", ("Vh0" if h == 0 else "Vh1", t, s))]
                    if s > 0:
                        w.append(("s_1p", ("kv", t, s - 1, 3 + 4 * h)))
                    elif t > 0:
                        w.append(("s_1p", ("kv", t - 1, 3, 3 + 4 * h)))

                    def vb(v, t=t, s=s, h=h):
                        return v.tensor_tensor(v_b[:, h * 512:(h + 1) * 512],
                                               p1v[:, h * 512:(h + 1) * 512],
                                               bvbc[:, h * 512:(h + 1) * 512], ALU.add)
                    op("vector", vb, waits=w, inc=("s_1d", ("vb0" if h == 0 else "vb1", t, s)))
                for half in range(2):
                    w = [("s_1p", ("kpA" if half == 0 else "kpB", t, s))]
                    if s > 0:
                        w.append(("s_1p", ("kv", t, s - 1, 3 + 4 * half)))
                    elif t > 0:
                        w.append(("s_1p", ("kv", t - 1, 3, 3 + 4 * half)))

                    def relu(v, t=t, s=s, half=half):
                        return v.tensor_scalar(kpbuf[:, half * 512:(half + 1) * 512],
                                               p1k[half][:, :],
                                               0.0, EPS_AT, ALU.max, ALU.add)
                    op("vector", relu, waits=w,
                       inc=("s_1d", ("reluA" if half == 0 else "reluB", t, s)))

            def dve_kq(t):
                for c in range(8):
                    def kqf(v, t=t, c=c):
                        return v.tensor_scalar(kq[:, c * 512:(c + 1) * 512], p1k[c % 2][:, :],
                                               bks[:, c:c + 1], None, ALU.add)
                    w = [("s_1p", ("K", t, c))]
                    if t > 0:
                        w.append(("s_1p", ("kpA" if c < 4 else "kpB", t - 1, 3)))
                    if t == 0 and c == 0:
                        w.append(("s_ad", ("bksev",)))
                    op("vector", kqf, waits=w, inc=("s_ad", ("kq", t, c)))

            hooks = {1: [1], 3: [2], 4: [22], 5: [3], 6: [4], 7: [5, 6]}
            for i in range(9):
                ln_grp(i)
                for st in hooks.get(i, []):
                    prep_d(st)
            dve_kq(0)

            def app31():
                j = NCH - 1
                parj = slice(j % 2, j % 2 + 1)

                def f(v, j=j, parj=parj):
                    return v.tensor_scalar(xnt[j % 2][:, :], xtm[j % 3][:, :],
                                           meanv[:, parj], rstd[:, parj],
                                           ALU.subtract, ALU.mult)
                op("vector", f, waits=[("s_aa", ("rstd", j)), ("s_ap", ("T", j - 2))],
                   inc=("s_ad", ("app", j)))

            for t in range(NT):
                p3_s(t, 0)
                for i in (4 * t + 9, 4 * t + 10):
                    if i < NCH:
                        ln_grp(i)
                p3_s(t, 1)
                for i in (4 * t + 11, 4 * t + 12):
                    if i < NCH:
                        ln_grp(i)
                        if i == NCH - 1:
                            app31()
                p3_s(t, 2)
                p3_s(t, 3)
                if t < NT - 1:
                    dve_kq(t + 1)
            vcopy(kvstage[:, 0:512], kvA[:, :], [("s_1p", ("kv", 7, 3, 7))], ("s_xd", ("kvA",)))
            vcopy(kvstage[:, 512:1024], kvB[:, :], [], ("s_xd", ("kvB",)))
            vcopy(kvstage[:, 1024:1032], kvS[:, 0:8], [], ("s_xd", ("kvst",)))
            for c in range(8):
                src = p1k[c % 2][:, :].bitcast(BF16).rearrange("p (cc t) -> p cc t", cc=8)
                wtr = wk[:, :].rearrange("p (cc o) -> p cc o", cc=8)

                def wkev0(v, c=c, src=src, wtr=wtr):
                    return v.tensor_copy(wtr[:, 0:4, c * 128:(c + 1) * 128], src[:, 0:4, :])

                def wkev1(v, c=c, src=src, wtr=wtr):
                    return v.tensor_copy(wtr[:, 4:8, c * 128:(c + 1) * 128], src[:, 4:8, :])
                op("vector", wkev0, waits=[("s_1p", ("wt", c))], inc=("s_1d", ("wkev0", c)))
                op("vector", wkev1, waits=[], inc=("s_1d", ("wkev", c)))

            def act_grp(i):
                par = slice(i % 2, i % 2 + 1)

                def sq(s_, i=i, par=par):
                    return s_.activation(sqsc[:, :], xtm[i % 3][:, :], AF.Square,
                                         accum_out=sumsq[:, par])
                w = [("s_ax", ("x", i))]
                if i >= 2:
                    w.append(("s_ad", ("var", i - 2)))
                op("scalar", sq, waits=w, inc=("s_aa", ("sq", i)))

                def lnf(s_, i=i, par=par):
                    return s_.activation(lnt[:, par], varv[:, par], AF.Ln, bias=cst[:, 0:1])
                w = [("s_ad", ("var", i))]
                if i == 0:
                    w.append(("s_cst", ("cst1",)))
                op("scalar", lnf, waits=w, inc=("s_aa", ("ln", i)))

                def rstdf(s_, i=i, par=par):
                    return s_.activation(rstd[:, par], lnt[:, par], AF.Exp, scale=-0.5)
                op("scalar", rstdf, waits=[("s_aa", ("ln", i))], inc=("s_aa", ("rstd", i)))
                if i >= 2:
                    act_ev(i - 2)

            def act_ev(i):
                def ev1(s_, i=i):
                    return s_.activation(xn_r[:, 0:4, i * 128:(i + 1) * 128],
                                         pab_r[:, 0:4, :], AF.Copy)

                def ev2(s_, i=i):
                    return s_.activation(xn_r[:, 4:8, i * 128:(i + 1) * 128],
                                         pab_r[:, 4:8, :], AF.Copy)
                op("scalar", ev1, waits=[("s_ap", ("T", i))], inc=("s_aa", ("ev1", i)))
                op("scalar", ev2, waits=[], inc=("s_aa", ("ev2", i)))

            for i in range(NCH):
                act_grp(i)
                if i == 7:
                    def tmp4f(s_):
                        return s_.activation(tmp4[:, :], s4m[:, :], AF.Ln)
                    op("scalar", tmp4f, waits=[("s_ad", ("s4m",))], inc=("s_aa", ("pln",)))

                    def rn4f(s_):
                        return s_.activation(rn4[:, :], tmp4[:, :], AF.Exp, scale=-0.5)
                    op("scalar", rn4f, waits=[("s_aa", ("pln",))], inc=("s_aa", ("prn4",)))

                    def sclEf(s_):
                        return s_.activation(sclE[:, :], sclm[:, :], AF.Exp)
                    op("scalar", sclEf, waits=[("s_ad", ("sclm",))], inc=("s_aa", ("pexp",)))
            act_ev(NCH - 2)
            act_ev(NCH - 1)

            def t_grp(i):
                def tr(t_, i=i):
                    for jj in range(8):
                        li = t_.transpose(pab[:, jj * 128:(jj + 1) * 128],
                                          xnt[i % 2][:, jj * 128:(jj + 1) * 128],
                                          identb[:, :])
                    return li
                w = [("s_ad", ("app", i))]
                if i >= 1:
                    w.append(("s_aa", ("ev2", i - 1)))
                if i == 0:
                    w.append(("s_pl", ("c", "identb")))
                op("tensor", tr, waits=w, inc=("s_ap", ("T", i)))

            def pe_prep_bvmm():
                def f(t_):
                    t_.matmul(B1[:, :], onesr_f[:, :], bv_row[0:1, 0:512], start=True, stop=True)
                    return t_.matmul(B2[:, :], onesr_f[:, :], bv_row[0:1, 512:1024],
                                     start=True, stop=True)
                op("tensor", f, waits=[("s_pl", ("c", "scl"))], inc=("s_ap", ("bvmm",)))

            def pe_wut(ocs):
                for oc in ocs:
                    def f(t_, oc=oc):
                        return t_.transpose(B2b[0:15, oc * 128:(oc + 1) * 128],
                                            wu_st[:, oc * 15:oc * 15 + 15],
                                            identb[:, :])
                    w = []
                    if oc == 0:
                        w = [("s_ag", ("wu", 7)), ("s_ad", ("bvbc",))]
                    op("tensor", f, waits=w, inc=("s_ap", ("wutT", oc)))

            def pe_wpt(pc):
                def f(t_, pc=pc):
                    bank = B1b if pc == 0 else B2b
                    for dc in range(8):
                        li = t_.transpose(bank[:, dc * 128:(dc + 1) * 128],
                                          qp_b[:, pc * 1024 + dc * 128: pc * 1024 + (dc + 1) * 128],
                                          identb[:, :])
                    return li
                if pc == 0:
                    w = [("s_ag", ("wp", 1)), ("s_ad", ("bvbc",))]
                else:
                    w = [("s_ad", ("wutev", 7))]
                op("tensor", f, waits=w, inc=("s_ap", (f"wpt{pc}",)))

            def pe_wdt():
                def f(t_):
                    for dc in range(8):
                        t_.transpose(B1b[:, dc * 16: dc * 16 + 15],
                                     wd_bf[0:15, dc * 128:(dc + 1) * 128], identb[0:15, 0:15])
                    t_.transpose(B2[:, 0:8], bq_st[:, :], identf[0:8, 0:8])
                    t_.transpose(B2[:, 8:16], bk_st[:, :], identf[0:8, 0:8])
                    return t_.transpose(B2[:, 16:18], bp2[:, :], identf[0:2, 0:2])
                op("tensor", f,
                   waits=[("s_ad", ("wpt0ev", 7)), ("s_ad", ("wpt1ev", 7)),
                          ("s_ag", ("wd",)), ("s_pl", ("c", "bp2"))],
                   inc=("s_ap", ("wdt",)))

            def pe_wps():
                def f(t_):
                    first = True
                    for dc in range(8):
                        for pc in range(2):
                            t_.matmul(B1[:, dc * 4:(dc + 1) * 4],
                                      qp_b[:, pc * 1024 + dc * 128: pc * 1024 + (dc + 1) * 128],
                                      sim_b[:, pc * 4:(pc + 1) * 4],
                                      start=first, stop=(dc == 7 and pc == 1))
                            first = False
                    t_.matmul(B2[0:1, 0:4], bp2t[:, 0:1], sim_f[:, 0:4], start=True, stop=False)
                    t_.matmul(B2[0:1, 0:4], bp2t[:, 1:2], sim_f[:, 4:8], start=False, stop=False)
                    t_.matmul(B2[0:1, 8:12], onesc_f[:, :], sim_sq[:, 0:4], start=False, stop=False)
                    return t_.matmul(B2[0:1, 8:12], onesc_f[:, :], sim_sq[:, 4:8],
                                     start=False, stop=True)
                op("tensor", f,
                   waits=[("s_ad", ("wdtev", 7)), ("s_ad", ("bp2tev",)),
                          ("s_ad", ("simc",)), ("s_ad", ("simsq",))],
                   inc=("s_ap", ("wps",)))

            def pe_rnscl():
                def f(t_):
                    t_.matmul(B1[:, 0:4], onesr_f[:, :], rn4[:, :], start=True, stop=False)
                    return t_.matmul(B1[:, 8:9], onesr_f[:, :], sclE[:, :], start=False, stop=True)
                op("tensor", f,
                   waits=[("s_aa", ("prn4",)), ("s_aa", ("pexp",)), ("s_ad", ("wpsev", 7))],
                   inc=("s_ap", ("rnscl",)))

            def pe_K(t, c):
                def f(t_, t=t, c=c):
                    for kc in range(8):
                        li = t_.matmul(p1k[c % 2][:, :],
                                       wk[:, kc * 1024 + c * 128: kc * 1024 + (c + 1) * 128],
                                       xn[:, kc * TPC + t * 512: kc * TPC + (t + 1) * 512],
                                       start=(kc == 0), stop=(kc == 7))
                    return li
                w = []
                if t == 0 and c == 0:
                    w += [("s_ad", ("prepdone",)), ("s_ag", ("wk", 7))]
                if t == 0 and c == 1:
                    w += [("s_ad", ("prepdone",))]
                if c == 0:
                    w.append(("s_aa", ("ev2", 4 * t + 3)))
                if c >= 2:
                    w.append(("s_ad", ("kq", t, c - 2)))
                elif t > 0:
                    w.append(("s_1d", ("reluA" if c == 0 else "reluB", t - 1, 3)))
                op("tensor", f, waits=w, inc=("s_1p", ("K", t, c)))

            def pe_V_half(t, s, h):
                def f(t_, t=t, s=s, h=h):
                    tok = t * 512 + s * 128
                    for kc in range(8):
                        li = t_.matmul(p1v[:, h * 512:(h + 1) * 512],
                                       xn[:, kc * TPC + tok: kc * TPC + tok + 128],
                                       wv[:, kc * 1024 + h * 512: kc * 1024 + (h + 1) * 512],
                                       start=(kc == 0), stop=(kc == 7))
                    return li
                w = []
                if t == 0 and s == 0 and h == 0:
                    w.append(("s_ag", ("wv", 7)))
                vbk = "vb0" if h == 0 else "vb1"
                if s > 0:
                    w.append(("s_1d", (vbk, t, s - 1)))
                elif t > 0:
                    w.append(("s_1d", (vbk, t - 1, 3)))
                op("tensor", f, waits=w, inc=("s_1p", ("Vh0" if h == 0 else "Vh1", t, s)))

            def pe_kp_half(t, s, half):
                def f(t_, t=t, s=s, half=half):
                    for pp in range(4):
                        p = 4 * half + pp
                        li = t_.matmul(p1k[half][:, pp * 128:(pp + 1) * 128],
                                       kq[:, p * 512 + s * 128: p * 512 + (s + 1) * 128],
                                       rw2[:, p * 128:(p + 1) * 128],
                                       start=True, stop=True)
                    return li
                w = [("s_ad", ("kq", t, 4 * half + 3))]
                if s > 0:
                    w.append(("s_1d", ("reluA" if half == 0 else "reluB", t, s - 1)))
                else:
                    w.append(("s_ad", ("kq", t, 6 + half)))
                if t == 0 and s == 0 and half == 0:
                    w += [("s_ag", ("rw", 15)), ("s_cst", ("msets",))]
                op("tensor", f, waits=w, inc=("s_1p", ("kpA" if half == 0 else "kpB", t, s)))

            def pe_kv(t, s, p):
                def f(t_, t=t, s=s, p=p):
                    kvb = kvA if p < 4 else kvB
                    col = (p % 4) * 128
                    fr = (t == 0 and s == 0)
                    last = (t == NT - 1 and s == 3)
                    t_.matmul(kvb[:, col:col + 128], kp_b[p][:, :],
                              v_b[:, p * 128:(p + 1) * 128],
                              start=(fr and p % 4 == 0), stop=(last and p % 4 == 3))
                    return t_.matmul(kvS[:, p:p + 1], kp_b[p][:, :], onesc_b[:, :],
                                     start=(fr and p == 0), stop=(last and p == 7))
                w = []
                if p % 4 == 0:
                    w = [("s_1d", ("reluA" if p < 4 else "reluB", t, s)),
                         ("s_1d", ("vb0" if p < 4 else "vb1", t, s))]
                op("tensor", f, waits=w, inc=("s_1p", ("kv", t, s, p)))

            def pe_wt(c):
                def f(t_, c=c):
                    dst = p1k[c % 2][:, :].bitcast(BF16)
                    for dd in range(8):
                        li = t_.transpose(dst[:, dd * 128:(dd + 1) * 128],
                                          qp_b[:, (c % 2) * 1024 + dd * 128:
                                               (c % 2) * 1024 + (dd + 1) * 128],
                                          identb[:, :])
                    return li
                w = [("s_ag", ("W", c))]
                if c < 2:
                    w.append(("s_1d", ("reluA" if c == 0 else "reluB", 7, 3)))
                else:
                    w.append(("s_1d", ("wkev", c - 2)))
                op("tensor", f, waits=w, inc=("s_1p", ("wt", c)))

            pe_prep_bvmm()
            pe_wut(range(4))
            t_grp(0)
            pe_wpt(0)
            t_grp(1)
            pe_wut(range(4, 8))
            t_grp(2)
            pe_wpt(1)
            t_grp(3)
            pe_wdt()
            t_grp(4)
            pe_wps()
            t_grp(5)
            pe_rnscl()
            t_grp(6)
            t_grp(7)
            def maybe_t(i):
                if i < NCH:
                    t_grp(i)

            for t in range(NT):
                for c in range(8):
                    pe_K(t, c)
                pe_V_half(t, 0, 0)
                pe_V_half(t, 0, 1)
                pe_kp_half(t, 0, 0)
                pe_kp_half(t, 0, 1)
                for s in range(4):
                    if s + 1 < 4:
                        pe_V_half(t, s + 1, 0)
                        pe_V_half(t, s + 1, 1)
                    for p in range(4):
                        pe_kv(t, s, p)
                    if s == 1:
                        maybe_t(4 * t + 8)
                    for p in range(4, 8):
                        pe_kv(t, s, p)
                    if s == 1:
                        maybe_t(4 * t + 9)
                    elif s == 2:
                        maybe_t(4 * t + 10)
                    if s + 1 < 4:
                        pe_kp_half(t, s + 1, 0)
                        pe_kp_half(t, s + 1, 1)
                    if s == 2:
                        maybe_t(4 * t + 11)
            for c in range(8):
                pe_wt(c)

            _DBG["streams"] = streams
            for en in streams:
                for (fn, waits, inc, by) in streams[en]:
                    if inc is not None:
                        cc = C[inc[0]]
                        cc["n"] += by
                        cc["m"][inc[1]] = cc["n"]

            _DBG["C"] = C

            def emit(en, engine):
                for (fn, waits, inc, by) in streams[en]:
                    for (sn, key) in waits:
                        engine.wait_ge(sem[sn], C[sn]["m"][key])
                    ins = fn(engine)
                    if inc is not None:
                        ins.then_inc(sem[inc[0]], by)

            with nc.Block() as blk:
                @blk.sync
                def _(sy):
                    emit("sync", sy)

                @blk.gpsimd
                def _(g):
                    emit("gpsimd", g)

                @blk.vector
                def _(v):
                    emit("vector", v)

                @blk.scalar
                def _(s_):
                    emit("scalar", s_)

                @blk.tensor
                def _(t_):
                    emit("tensor", t_)

        s_2p, s_2d, s_2a = sem["s_2p"], sem["s_2d"], sem["s_2a"]
        s_2x, s_2s = sem["s_2x"], sem["s_2s"]
        s_xd, s_xg, s_cc = sem["s_xd"], sem["s_xg"], sem["s_cc"]
        XD_MEGA = C["s_xd"]["n"]
        XG_MEGA = C["s_xg"]["n"]

        nt4 = NT
        with nc.psum_tensor("P0", [128, 512], F32) as P0, \
             nc.psum_tensor("P1", [128, 512], F32) as P1, \
             nc.psum_tensor("P2", [128, 512], F32) as P2, \
             nc.psum_tensor("P3", [128, 512], F32) as P3, \
             nc.psum_tensor("P4", [128, 512], F32) as P4, \
             nc.psum_tensor("P5", [128, 512], F32) as P5, \
             nc.psum_tensor("PASM", [128, 1024], F32) as PASM:
            PQ = [P0, P1]
            PR = [P3, P4]
            PPRJ = [P2, P5]

            D2T, A2T, P2T = 98, 24, 45

            def d2(t, k):
                return D2T * t + k

            def a2(t, k):
                return A2T * t + k

            def p2(t, k):
                return P2T * t + k

            def dQ(c):
                return c + 1

            def dqp(p):
                return 8 + p + 1

            D_RN1, D_RN2 = 17, 18

            def ddiv(c):
                return 18 + c + 1

            def dg(tt, k):
                return 26 + 18 * tt + k

            _PQ = {0: 1, 1: 2, 2: 5, 3: 6, 4: 7, 5: 8, 6: 9, 7: 10}

            def pQ(c):
                return _PQ[c]

            def pqp(p):
                return 10 + p + 1

            P_NORM = 19

            def pqkv(c):
                return 19 + 2 * c + 1

            def prnb(c):
                return 19 + 2 * c + 2

            _PPRJ = {0: 36, 1: 37, 2: 40, 3: 43}

            def pprj(tt):
                return _PPRJ[tt]

            def ptpt_abs(t3, tt):
                if tt < 3:
                    return p2(t3, {0: 38, 1: 41, 2: 44}[tt])
                return p2(t3 + 1, 3) if t3 < nt4 - 1 else p2(nt4, 1)

            def pasm_abs(t3, tt):
                if tt < 3:
                    return p2(t3, {0: 39, 1: 42, 2: 45}[tt])
                return p2(t3 + 1, 4) if t3 < nt4 - 1 else p2(nt4, 2)

            def aRnb(c):
                return c + 1

            def ag(tt, k):
                return 8 + 4 * tt + k

            def asm_banks(tt):
                if tt % 2 == 1:
                    return PASM[:, 0:512], PASM[:, 512:1024]
                return P0[:, :], P1[:, :]

            with nc.Block() as blk:
                @blk.tensor
                def _(t_):
                    def asm_close3_prev(tp):
                        asmL, asmR = asm_banks(3)
                        t_.wait_ge(s_2d, d2(tp, dg(3, 15)))
                        t_.transpose(P3[:, :].bitcast(BF16)[0:16, 0:128], t1b[:, :], identb[:, :]).then_inc(s_2p, 1)
                        t_.wait_ge(s_2d, d2(tp, dg(3, 16)))
                        t_.matmul(asmL, tpt[:, :], wut[:, 0:512], start=False, stop=True)
                        t_.matmul(asmR, tpt[:, :], wut[:, 512:1024], start=False, stop=True).then_inc(s_2p, 1)

                    for t in range(nt4):
                        for c in range(8):
                            if c == 2 and t == 0:
                                t_.transpose(P3[:, :].bitcast(BF16)[0:16, 0:128], t1b[:, :], identb[:, :]).then_inc(s_2p, 1)
                                t_.transpose(P3[:, :].bitcast(BF16)[0:16, 0:128], t1b[:, :], identb[:, :]).then_inc(s_2p, 1)
                            if c == 2 and t > 0:
                                asm_close3_prev(t - 1)
                            if c >= 2:
                                t_.wait_ge(s_2d, d2(t, dQ(c - 2)))
                            elif t > 0:
                                t_.wait_ge(s_2d, d2(t - 1, dg(2, 18)))
                            for kc in range(8):
                                li = t_.matmul(PQ[c % 2][:, :],
                                          wq[:, kc * 1024 + c * 128: kc * 1024 + (c + 1) * 128],
                                          xn[:, kc * TPC + t * 512: kc * TPC + (t + 1) * 512],
                                          start=(kc == 0), stop=(kc == 7))
                            li.then_inc(s_2p, 1)
                        for p in range(8):
                            t_.wait_ge(s_2d, d2(t, 8))
                            if p >= 2:
                                t_.wait_ge(s_2d, d2(t, dqp(p - 2)))
                            t_.matmul(PQ[p % 2][:, :], rw2[:, p * 128:(p + 1) * 128],
                                      kq[:, p * 512:(p + 1) * 512], start=True, stop=True).then_inc(s_2p, 1)
                        if t == 0:
                            t_.wait_ge(s_xd, XD_MEGA + 1)
                        for p in range(8):
                            t_.wait_ge(s_2d, d2(t, dqp(p)))
                            li = t_.matmul(P2[0:16, :], kbd[:, p * 16:(p + 1) * 16],
                                      qp_b[:, p * 512:(p + 1) * 512],
                                      start=(p == 0), stop=(p == 7))
                        li.then_inc(s_2p, 1)
                        for c in range(8):
                            if c < 2:
                                t_.wait_ge(s_2d, d2(t, dqp(6 + c)))
                            else:
                                t_.wait_ge(s_2d, d2(t, ddiv(c - 2)))
                            t_.matmul(PQ[c % 2][:, :], kv2[:, c * 128:(c + 1) * 128],
                                      qp_b[:, c * 512:(c + 1) * 512], start=True, stop=True).then_inc(s_2p, 1)
                            t_.wait_ge(s_2d, d2(t, D_RN2))
                            if c >= 2:
                                t_.wait_ge(s_2a, a2(t, aRnb(c - 2)))
                            t_.matmul(PR[c % 2][:, :], sel16[:, c * 128:(c + 1) * 128],
                                      rn_b[:, :], start=True, stop=True).then_inc(s_2p, 1)
                        def asm_base(tt):
                            j = 4 * t + tt
                            asmL, asmR = asm_banks(tt)
                            prj = PPRJ[tt % 2]
                            if j >= 2:
                                t2_, tt2 = (j - 2) // 4, (j - 2) % 4
                                t_.wait_ge(s_2d, d2(t2_, dg(tt2, 18)))
                            if tt % 2 == 0:
                                t_.wait_ge(s_2d, d2(t, ddiv(7)))
                            if tt == 0:
                                t_.wait_ge(s_2d, d2(t, D_RN2))
                                if t > 0:
                                    t_.wait_ge(s_2d, d2(t - 1, dg(2, 15)))
                            elif tt == 1:
                                if t > 0:
                                    t_.wait_ge(s_2d, d2(t - 1, dg(3, 15)))
                            else:
                                t_.wait_ge(s_2d, d2(t, dg(tt - 2, 15)))
                            for kc in range(8):
                                t_.matmul(asmL, xn[:, kc * TPC + j * 128: kc * TPC + (j + 1) * 128],
                                          wk[:, kc * 1024: kc * 1024 + 512], start=(kc == 0), stop=False)
                                t_.matmul(asmR, xn[:, kc * TPC + j * 128: kc * TPC + (j + 1) * 128],
                                          wk[:, kc * 1024 + 512: (kc + 1) * 1024], start=(kc == 0), stop=False)
                                t_.matmul(prj[:, 0:WPD], xn[:, kc * TPC + j * 128: kc * TPC + (j + 1) * 128],
                                          wpd[:, kc * WPD: (kc + 1) * WPD], start=(kc == 0), stop=False)
                            t_.matmul(prj[:, 0:WPD], onesr_b[:, :], bp_row[:, :], start=False, stop=True).then_inc(s_2p, 1)
                            t_.matmul(asmL, onesr_b[:, :], bo_row[0:1, 0:512], start=False, stop=False)
                            t_.matmul(asmR, onesr_b[:, :], bo_row[0:1, 512:1024], start=False, stop=False)
                            for kc in range(8):
                                t_.wait_ge(s_2d, d2(t, ddiv(kc)))
                                t_.matmul(asmL, o_b[:, kc * 512 + tt * 128: kc * 512 + (tt + 1) * 128],
                                          wo[:, kc * 1024: kc * 1024 + 512], start=False, stop=False)
                                t_.matmul(asmR, o_b[:, kc * 512 + tt * 128: kc * 512 + (tt + 1) * 128],
                                          wo[:, kc * 1024 + 512: (kc + 1) * 1024], start=False, stop=False)

                        def asm_close(tt):
                            asmL, asmR = asm_banks(tt)
                            t_.wait_ge(s_2d, d2(t, dg(tt, 15)))
                            if tt == 0:
                                t_.wait_ge(s_2a, a2(t, aRnb(7)))
                            t_.transpose(P3[:, :].bitcast(BF16)[0:16, 0:128], t1b[:, :], identb[:, :]).then_inc(s_2p, 1)
                            t_.wait_ge(s_2d, d2(t, dg(tt, 16)))
                            t_.matmul(asmL, tpt[:, :], wut[:, 0:512], start=False, stop=True)
                            t_.matmul(asmR, tpt[:, :], wut[:, 512:1024], start=False, stop=True).then_inc(s_2p, 1)

                        asm_base(0)
                        asm_base(1)
                        asm_close(0)
                        asm_base(2)
                        asm_close(1)
                        asm_base(3)
                        asm_close(2)
                    asm_close3_prev(nt4 - 1)

                @blk.vector
                def _(v):
                    for t in range(nt4):
                        for c in range(8):
                            v.wait_ge(s_2p, p2(t, pQ(c)))
                            v.tensor_scalar(kq[:, c * 512:(c + 1) * 512], PQ[c % 2][:, :],
                                            bqs[:, c:c + 1], None, ALU.add).then_inc(s_2d, 1)
                        for p in range(8):
                            v.wait_ge(s_2p, p2(t, pqp(p)))
                            v.tensor_scalar(qp_b[:, p * 512:(p + 1) * 512], PQ[p % 2][:, :],
                                            0.0, EPS_AT, ALU.max, ALU.add).then_inc(s_2d, 1)
                        if t == 0:
                            v.wait_ge(s_xg, XG_MEGA + 16)
                            for p in range(8):
                                v.tensor_copy(kv2[0:64, p * 128: p * 128 + 64], kvpart[0:64, p * 128: p * 128 + 64])
                                v.tensor_copy(kv2[64:128, p * 128 + 64: p * 128 + 128], kvpart[64:128, p * 128 + 64: p * 128 + 128])
                                v.tensor_copy(kbd[0:64, p * 16 + 2 * p: p * 16 + 2 * p + 1], kvpart[0:64, 1024 + p: 1025 + p])
                                lastc = v.tensor_copy(kbd[64:128, p * 16 + 2 * p + 1: p * 16 + 2 * p + 2], kvpart[64:128, 1024 + p: 1025 + p])
                            lastc.then_inc(s_xd, 1)
                        v.wait_ge(s_2p, p2(t, P_NORM))
                        v.tensor_scalar(norm_s[:, :], P2[0:16, :], EPS_AT, None, ALU.add).then_inc(s_2d, 1)
                        v.wait_ge(s_2d, d2(t, D_RN1))
                        with nc.allow_low_precision(reason="rn feeds a bf16 matmul operand"):
                            v.reciprocal(rn_b[:, :], norm_s[:, :]).then_inc(s_2d, 1)
                        for c in range(8):
                            v.wait_ge(s_2p, p2(t, pqkv(c)))
                            v.wait_ge(s_2a, a2(t, aRnb(c)))
                            v.tensor_tensor(o_b[:, c * 512:(c + 1) * 512], PQ[c % 2][:, :],
                                            rnb_sb[c % 2][:, :], ALU.mult).then_inc(s_2d, 1)
                        for tt in range(4):
                            j = 4 * t + tt
                            prj = PPRJ[tt % 2]
                            asmL, asmR = asm_banks(tt)
                            v.wait_ge(s_2a, a2(t, ag(tt, 1)))
                            v.tensor_scalar(m1[:, :], ssq[:, :], 1e-24, None, ALU.max).then_inc(s_2d, 1)
                            v.wait_ge(s_2a, a2(t, ag(tt, 3)))
                            v.tensor_scalar(rns[:, :], rnorm[:, :], sclbc[:, 0:1], None, ALU.mult).then_inc(s_2d, 1)
                            v.tensor_tensor(lg1[:, :], prj[:, 271:275], rn4bc[:, :], ALU.mult).then_inc(s_2d, 1)
                            v.wait_ge(s_2d, d2(t, dg(tt, 3)))
                            v.tensor_scalar(logits[:, :], lg1[:, :], rns[:, 0:1], None, ALU.mult).then_inc(s_2d, 1)
                            v.wait_ge(s_2d, d2(t, dg(tt, 4)))
                            v.reduce_max(nmx[:, :], logits[:, :], axis=AX, negate=True).then_inc(s_2d, 1)
                            v.wait_ge(s_2a, a2(t, ag(tt, 4)))
                            v.reduce_sum(sev[:, :], exps[:, :], axis=AX).then_inc(s_2d, 1)
                            v.wait_ge(s_2d, d2(t, dg(tt, 6)))
                            v.reciprocal(rse[:, :], sev[:, :]).then_inc(s_2d, 1)
                            v.wait_ge(s_2d, d2(t, dg(tt, 7)))
                            v.tensor_scalar(lw[:, :], exps[:, :], rse[:, 0:1], None, ALU.mult).then_inc(s_2d, 1)
                            v.wait_ge(s_2d, d2(t, dg(tt, 8)))
                            v.reduce_max(mxv[:, :], lw[:, :], axis=AX).then_inc(s_2d, 1)
                            v.wait_ge(s_2d, d2(t, dg(tt, 9)))
                            v.tensor_scalar(maskv[:, :], lw[:, :], mxv[:, 0:1], None, ALU.is_ge).then_inc(s_2d, 1)
                            v.wait_ge(s_2d, d2(t, dg(tt, 10)))
                            v.tensor_mul(retv[:, :], lw[:, :], maskv[:, :]).then_inc(s_2d, 1)
                            v.wait_ge(s_2d, d2(t, dg(tt, 11)))
                            if j > 0:
                                t3, tt3 = (j - 1) // 4, (j - 1) % 4
                                v.wait_ge(s_2p, ptpt_abs(t3, tt3))
                            off = 0
                            for ei, r in enumerate([1, 2, 4, 8]):
                                v.tensor_tensor(t1b[:, off:off + r], prj[:, 256 + off:256 + off + r],
                                                retv[:, ei:ei + 1].broadcast_to([128, r]), ALU.mult).then_inc(s_2d, 1)
                                off += r
                            v.wait_ge(s_2p, ptpt_abs(t, tt))
                            v.tensor_copy(tpt[:, :], P3[:, :].bitcast(BF16)[0:16, 0:128]).then_inc(s_2d, 1)
                            v.wait_ge(s_2p, pasm_abs(t, tt))
                            v.wait_ge(s_2x, 16 * (j + 1))
                            if j >= 2:
                                v.wait_ge(s_2s, 16 * (j - 1))
                            if j == 0:
                                v.wait_ge(s_xg, XG_MEGA + 16)
                            if j == 1:
                                v.wait_ge(s_xd, XD_MEGA + 1)
                            v.tensor_tensor(out_tm[j % 2][:, 0:512], asmL, xtm[j % 3][:, 0:512], ALU.add).then_inc(s_2d, 1)
                            v.tensor_tensor(out_tm[j % 2][:, 512:1024], asmR, xtm[j % 3][:, 512:1024], ALU.add).then_inc(s_2d, 1)

                @blk.scalar
                def _(s):
                    for t in range(nt4):
                        for c in range(8):
                            s.wait_ge(s_2p, p2(t, prnb(c)))
                            if c >= 2:
                                s.wait_ge(s_2d, d2(t, ddiv(c - 2)))
                            s.activation(rnb_sb[c % 2][:, :], PR[c % 2][:, :], AF.Copy).then_inc(s_2a, 1)
                        for tt in range(4):
                            j = 4 * t + tt
                            prj = PPRJ[tt % 2]
                            s.wait_ge(s_2p, p2(t, pprj(tt)))
                            if j > 0:
                                t3, tt3 = (j - 1) // 4, (j - 1) % 4
                                s.wait_ge(s_2d, d2(t3, dg(tt3, 1)))
                            s.activation(sqg[:, :], prj[:, 0:256], AF.Square, accum_out=ssq[:, :]).then_inc(s_2a, 1)
                            s.wait_ge(s_2d, d2(t, dg(tt, 1)))
                            s.activation(lnp[:, :], m1[:, :], AF.Ln).then_inc(s_2a, 1)
                            s.wait_ge(s_2a, a2(t, ag(tt, 2)))
                            if j > 0:
                                t3, tt3 = (j - 1) // 4, (j - 1) % 4
                                s.wait_ge(s_2d, d2(t3, dg(tt3, 2)))
                            s.activation(rnorm[:, :], lnp[:, :], AF.Exp, scale=-0.5).then_inc(s_2a, 1)
                            s.wait_ge(s_2d, d2(t, dg(tt, 5)))
                            if j > 0:
                                t3, tt3 = (j - 1) // 4, (j - 1) % 4
                                s.wait_ge(s_2d, d2(t3, dg(tt3, 8)))
                            s.activation(exps[:, :], logits[:, :], AF.Exp, bias=nmx[:, 0:1]).then_inc(s_2a, 1)

                @blk.sync
                def _(sy):
                    for j in range(4 * nt4):
                        if j >= 3:
                            t3, tt3 = (j - 3) // 4, (j - 3) % 4
                            sy.wait_ge(s_2d, d2(t3, dg(tt3, 18)))
                        sy.dma_start(out=xtm[j % 3][:, :], in_=xh[j * 128:(j + 1) * 128, :]).then_inc(s_2x, 16)

                @blk.gpsimd
                def _(g):
                    g.wait_ge(s_cc, 1)
                    g.dma_start(out=kvpart[:, :], in_=ccout[:, :]).then_inc(s_xg, 16)
                    for j in range(4 * nt4):
                        t3, tt3 = j // 4, j % 4
                        g.wait_ge(s_2d, d2(t3, dg(tt3, 18)))
                        g.dma_start(out=yh[j * 128:(j + 1) * 128, :], in_=out_tm[j % 2][:, 0:1024]).then_inc(s_2s, 16)
                    g.wait_ge(s_2s, 16 * 4 * nt4)

    return nc


def _consts():
    c = {}
    c["identb"] = np.eye(128, dtype=ml_dtypes.bfloat16)
    c["identf"] = np.eye(128, dtype=np.float32)
    c["onesr_b"] = np.ones((1, 128), dtype=ml_dtypes.bfloat16)
    c["onesr_f"] = np.ones((1, 128), dtype=np.float32)
    c["onesc_b"] = np.ones((128, 1), dtype=ml_dtypes.bfloat16)
    c["onesc_f"] = np.ones((128, 1), dtype=np.float32)
    sel = np.zeros((16, 1024), dtype=ml_dtypes.bfloat16)
    for ch in range(8):
        for jj in range(128):
            h = 2 * ch + (1 if jj >= 64 else 0)
            sel[h, ch * 128 + jj] = 1
    c["sel16"] = sel
    return c


def kernel(**inputs):
    x = np.ascontiguousarray(np.asarray(inputs["L0_attn_x"], dtype=np.float32))
    wd = np.ascontiguousarray(np.concatenate(
        [np.asarray(inputs[f"L0_ffn_Wd{i}"], dtype=np.float32) for i in range(4)], axis=0))
    wu = np.ascontiguousarray(np.concatenate(
        [np.asarray(inputs[f"L0_ffn_Wu{i}"], dtype=np.float32) for i in range(4)], axis=1))
    common = dict(
        wq=np.ascontiguousarray(inputs["L0_attn_Wq"]),
        wk=np.ascontiguousarray(inputs["L0_attn_Wk"]),
        wv=np.ascontiguousarray(inputs["L0_attn_Wv"]),
        wo=np.ascontiguousarray(inputs["L0_attn_Wo"]),
        bq=np.ascontiguousarray(inputs["L0_attn_bq"]),
        bk=np.ascontiguousarray(inputs["L0_attn_bk"]),
        bv=np.ascontiguousarray(inputs["L0_attn_bv"]),
        bo=np.ascontiguousarray(inputs["L0_attn_bo"]),
        rw=np.ascontiguousarray(inputs["L0_attn_rw"]),
        w=np.ascontiguousarray(inputs["L0_ffn_W"]),
        wp=np.ascontiguousarray(inputs["L0_ffn_Wp"]),
        bp=np.ascontiguousarray(inputs["L0_ffn_bp"]),
        sim=np.ascontiguousarray(inputs["L0_ffn_sim"]),
        temp=np.ascontiguousarray(inputs["L0_ffn_temp"]),
        wd=wd, wu=wu,
    )
    common.update(_consts())

    if "nc" not in _CACHE:
        _CACHE["nc"] = build_nc()
    nc = _CACHE["nc"]

    in_maps = []
    for c in range(8):
        b, h = c // 2, c % 2
        m = dict(common)
        m["x"] = np.ascontiguousarray(x[b, h * TPC:(h + 1) * TPC, :])
        in_maps.append(m)

    res = run_bass_kernel_spmd(nc, in_maps, core_ids=list(range(8)))
    out = np.empty((4, 8192, D), dtype=np.float32)
    for c in range(8):
        b, h = c // 2, c % 2
        out[b, h * TPC:(h + 1) * TPC, :] = res.results[c]["y"]
    return out
